# version 17
# baseline (speedup 1.0000x reference)
import numpy as np
import concourse.bass as bass
import concourse.mybir as mybir
from concourse.bass_utils import run_bass_kernel_spmd

F32 = mybir.dt.float32
BF16 = mybir.dt.bfloat16
F16 = mybir.dt.float16
AF = mybir.ActivationFunctionType
ALU = mybir.AluOpType
AX = mybir.AxisListType

D = 2048
TFULL = 4096
CH = 8
T = 512
NT = T // 128
NK = D // 128
FHALF = 5632
NJ = FHALF // 512
NKF = FHALF // 128
MAGIC = 1536.0


class Sem:
    def __init__(self, nc, name):
        self.h = nc.alloc_semaphore(name)
        self.n = 0


class Prog:
    def __init__(self):
        self.streams = {k: [] for k in "TVSQG"}
        self.side = []
        self.splice_idx = None

    def em(self, eng, fn):
        if eng == "QPRE":
            self.side.append(fn)
        else:
            self.streams[eng].append(fn)

    def mark_splice(self):
        self.splice_idx = len(self.streams["Q"])

    def do_splice(self):
        i = self.splice_idx if self.splice_idx is not None else 0
        self.streams["Q"] = (self.streams["Q"][:i] + self.side +
                             self.streams["Q"][i:])
        self.side = []


class Ring:

    SLOTS = 3

    def __init__(self, P, engs, nc, name):
        self.P = P
        self.engs = engs if isinstance(engs, (list, tuple)) \
            else [engs] * self.SLOTS
        self.sems = [Sem(nc, f"{name}_s{i}") for i in range(self.SLOTS)]
        self.queue = []
        self.seq_load = 0
        self.seq_marked = 0
        self.marks = {}
        self.slots = self.SLOTS

    def push(self, fn):
        self.queue.append(fn)

    def pump(self):
        while self.queue and self.seq_load < self.seq_marked + self.slots:
            fn = self.queue.pop(0)
            seq = self.seq_load
            slot = seq % self.slots
            prior = self.marks.pop(seq - self.slots, None)
            if prior is not None:
                s, c = prior
                self.P.em(self.engs[slot],
                          lambda e, s=s, c=c: e.wait_ge(s.h, c))
            fn(self.engs[slot], slot, self.sems[slot])
            self.sems[slot].n += 16
            self.seq_load += 1

    def mark(self, sem):
        self.marks[self.seq_marked] = (sem, sem.n)
        self.seq_marked += 1
        self.pump()

    def emit_wait(self, eng, seq):
        sem = self.sems[seq % self.slots]
        val = 16 * (seq // self.slots + 1)
        self.P.em(eng, lambda e, s=sem, v=val: e.wait_ge(s.h, v))


def build_program(debug_dump=False, chunks=CH):
    nc = bass.Bass("TRN2", target_bir_lowering=False, debug=False,
                   detect_race_conditions=False)

    x_d = nc.dram_tensor("x", [TFULL, D], F32, kind="ExternalInput").ap()
    winT_d = nc.dram_tensor("w_inT", [D, D], F32, kind="ExternalInput").ap()
    wgT_d = nc.dram_tensor("w_gT", [D, D], F32, kind="ExternalInput").ap()
    w13T_d = nc.dram_tensor("w13T", [D, 2 * FHALF], F32, kind="ExternalInput").ap()
    w2T_d = nc.dram_tensor("w2T", [FHALF, D], F32, kind="ExternalInput").ap()
    bin_d = nc.dram_tensor("b_in_col", [128, NK], F32, kind="ExternalInput").ap()
    bg_d = nc.dram_tensor("b_g_col", [128, NK], F32, kind="ExternalInput").ap()
    lam_d = nc.dram_tensor("lam_col", [128, NK], F32, kind="ExternalInput").ap()
    nw_d = nc.dram_tensor("normw_rep_in", [128, FHALF], F32, kind="ExternalInput").ap()
    id32_d = nc.dram_tensor("ident32", [128, 128], F32, kind="ExternalInput").ap()
    id16_d = nc.dram_tensor("ident16", [128, 128], BF16, kind="ExternalInput").ap()
    out_d = nc.dram_tensor("out", [TFULL, D], F32, kind="ExternalOutput").ap()
    dbg = {}
    if debug_dump:
        for nm, shp in (("dbg_xT", [128, NK * T]), ("dbg_y1", [128, NT * D]),
                        ("dbg_q1F", [128, NK * T]), ("dbg_gnw", [128, NJ * 4 * 512]),
                        ("dbg_iqT", [128, NKF * 512]), ("dbg_r2", [128, NT * D]),
                        ("dbg_sig", [128, 10 * T]),
                        ("dbg_y0", [128, NT * D])):
            dbg[nm] = nc.dram_tensor(nm, shp, F32, kind="ExternalOutput").ap()

    sg_d = nc.dram_tensor("s_gates", [NK, 128, NK * 4 * 128], BF16).ap()
    s13_d = nc.dram_tensor("s_w13", [NJ, NK, 128, 4 * 512], BF16).ap()
    s2_d = nc.dram_tensor("s_w2", [2, NKF, 128, 2048], BF16).ap()

    def pin(name, shape, dtype, off):
        return nc.alloc_sbuf_tensor_at(name, shape, dtype, offset=off).ap()

    BASE = 17408
    xstage = pin("xstage", [128, 2 * D], F32, BASE + 65536)
    sqscr = pin("sqscr", [128, D], F32, BASE)
    q1f16 = pin("q1f16", [128, D], F16, BASE + 8192)
    q1tm = pin("q1tm", [128, NT * D], F32, BASE + 16384)
    xTa = pin("xTa", [128, 8 * T], F32, BASE + 49152)
    xTb = pin("xTb", [128, 8 * T], F32, BASE + 131072)
    xhi = pin("xhi", [128, NK * T], BF16, BASE + 65536)
    xlo = pin("xlo", [128, NK * T], BF16, BASE + 81920)
    gnw = pin("gnw", [128, NJ * 4 * 512], F32, BASE)
    iqT = pin("iqT", [128, NKF * 512], BF16, BASE)
    sqscr2 = pin("sqscr2", [128, D], F32, BASE + 16384)
    drbuf = [pin(f"dr{i}", [128, 512], F32, BASE + 90112 + i * 2048)
             for i in range(2)]
    q2bf = [pin(f"q2bf{i}", [128, 512], F32,
                BASE + (94208 + i * 2048 if i < 2 else 90112 + (i - 2) * 2048))
            for i in range(4)]
    prew = [pin(f"prew{i}", [128, D], F32, BASE + i * 8192) for i in range(2)]
    prehi = [pin(f"prehi{i}", [128, D], BF16, BASE + 16384 + i * 4096)
             for i in range(2)]
    prelo = [pin(f"prelo{i}", [128, D], BF16, BASE + 24576 + i * 4096)
             for i in range(2)]

    def xT_sl(k, o0, ln):
        t_ = xTa if k < 8 else xTb
        return t_[:, (k % 8) * T + o0:(k % 8) * T + o0 + ln]

    y_t = pin("y", [128, NT * D], F32, BASE + 98304)
    q1F = pin("q1F", [128, NK * T], BF16, BASE + 131072)
    gring = [pin(f"gring{i}", [128, 2048], BF16, BASE + 147456 + i * 4096)
             for i in range(3)]
    wring = [pin(f"wring{i}", [128, 2048], BF16, BASE + 159744 + i * 4096)
             for i in range(3)]
    nwrep = pin("nwrep", [128, FHALF], F32, BASE + 172032)

    sb = nc.alloc_sbuf_tensor

    def right(name, shape, dtype):
        return sb(name, shape, dtype, side="right").ap()

    id32 = right("id32", [128, 128], F32)
    id16 = right("id16", [128, 128], BF16)
    a_e = right("a_e", [128, NK], F32)
    nbin = right("nbin", [128, NK], F32)
    nbg = right("nbg", [128, NK], F32)
    lamc = right("lamc", [128, NK], F32)
    carry = right("carry", [128, NK], F32)
    consts = right("consts", [128, 8], F32)
    ssum = right("ssum", [128, NT], F32)
    ssq = right("ssq", [128, NT], F32)
    smean = right("smean", [128, NT], F32)
    svar = right("svar", [128, NT], F32)
    sinv = right("sinv", [128, NT], F32)
    snmi = right("snmi", [128, NT], F32)
    sa1 = right("sa1", [128, NT], F32)
    ss1 = right("ss1", [128, NT], F32)
    srs1 = right("srs1", [128, NT], F32)
    sms = right("sms", [128, NT], F32)
    samax = right("samax", [128, NT], F32)
    srstd = right("srstd", [128, NT], F32)
    ss2c = right("ss2c", [128, NT], F32)
    srs2 = right("srs2", [128, NT], F32)
    srr2 = right("srr2", [128, NT], F32)
    stt1 = right("stt1", [128, NT], F32)
    sumb = right("sumb", [128, NKF], F32)
    maxb = right("maxb", [128, NKF], F32)
    bufA = right("bufA", [128, T], F32)
    bufB = right("bufB", [128, T], F32)
    bufC = right("bufC", [128, T], F32)
    bufD = right("bufD", [128, T], F32)
    bufE = right("bufE", [128, T], F32)
    bufDhold = [right(f"bufDh{i}", [128, T], F32) for i in range(2)]
    q2f16 = right("q2f16", [128, 512], F16)

    pb = [nc.alloc_psum_tensor(f"pb{i}", [128, 512], F32).ap() for i in range(8)]

    d_set = Sem(nc, "d_set")
    d_x = Sem(nc, "d_x")
    d_out = Sem(nc, "d_out")
    d_pl = [Sem(nc, f"d_pl{i}") for i in range(2)]
    d_pw = [Sem(nc, f"d_pw{i}") for i in range(2)]
    d_pwq = [Sem(nc, f"d_pwq{i}") for i in range(2)]
    sT = Sem(nc, "sT")
    sV = Sem(nc, "sV")
    sS = Sem(nc, "sS")

    P = Prog()
    gRing = Ring(P, ["Q", "Q", "Q"], nc, "gs")
    wRing = Ring(P, ["Q", "Q", "Q"], nc, "ws")

    bank_free = [None] * 8

    def t_wait_bank(b):
        bf = bank_free[b]
        if bf is not None:
            s, c = bf
            P.em("T", lambda e, s=s, c=c: e.wait_ge(s.h, c))

    inc_mark = {}

    def _inc(eng, sem):
        key = (eng, len(P.streams[eng]) - 1, sem)
        if inc_mark.get((eng, len(P.streams[eng]) - 1)) is sem:
            return sem.n
        sem.n += 1
        inc_mark[(eng, len(P.streams[eng]) - 1)] = sem
        prev = P.streams[eng][-1]

        def wrapped(e, prev=prev, s=sem):
            return prev(e).then_inc(s.h, 1)

        P.streams[eng][-1] = wrapped
        return sem.n

    def v_inc():
        return _inc("V", sV)

    def s_inc():
        return _inc("S", sS)

    def t_inc():
        return _inc("T", sT)

    def v_sync():
        v = v_inc()
        P.em("V", lambda e, v=v: e.wait_ge(sV.h, v))

    def s_sync():
        v = s_inc()
        P.em("S", lambda e, v=v: e.wait_ge(sS.h, v))

    def setup():
        def vsetup(e):
            e.memset(consts[:, 0:1], -1.0)
            e.memset(consts[:, 1:2], 0.5)
            e.memset(consts[:, 2:3], -0.5)
            e.memset(consts[:, 3:4], 1e-6)
            e.memset(consts[:, 4:5], 1e-5)
            e.memset(consts[:, 5:6], 1.0 / FHALF)
            e.memset(consts[:, 6:7], -8.0)
            e.memset(consts[:, 7:8], 1.0)
            return e.memset(carry, 0.0)
        P.em("V", vsetup)
        msv = v_inc()

        def gsetup(e):
            e.dma_start(out=id32, in_=id32_d).then_inc(d_set.h, 16)
            e.dma_start(out=id16, in_=id16_d).then_inc(d_set.h, 16)
            e.dma_start(out=nbin, in_=bin_d).then_inc(d_set.h, 16)
            e.dma_start(out=nbg, in_=bg_d).then_inc(d_set.h, 16)
            e.dma_start(out=lamc, in_=lam_d).then_inc(d_set.h, 16)
            return e.dma_start(out=nwrep, in_=nw_d).then_inc(d_set.h, 16)
        P.em("G", gsetup)
        d_set.n += 96
        wv = d_set.n

        P.em("V", lambda e, w=wv: e.wait_ge(d_set.h, w))
        P.em("V", lambda e: e.tensor_scalar(nbin, nbin, -1.0, None, ALU.mult))
        P.em("V", lambda e: e.tensor_scalar(nbg, nbg, -1.0, None, ALU.mult))
        nbv = v_inc()
        P.em("S", lambda e, w=wv: e.wait_ge(d_set.h, w))
        P.em("S", lambda e, v=msv: e.wait_ge(sV.h, v))
        P.em("S", lambda e: e.activation(a_e, lamc, AF.Exp))
        s_sync()
        P.em("S", lambda e: e.activation(a_e, a_e, AF.Ln,
                                         bias=consts[:, 7:8]))
        sfv = s_inc()
        P.em("V", lambda e, w=sfv: e.wait_ge(sS.h, w))
        P.em("V", lambda e: e.tensor_scalar(a_e, a_e, -8.0, None, ALU.mult))
        aev = v_inc()
        P.em("S", lambda e, v=aev: e.wait_ge(sV.h, v))
        P.em("S", lambda e, v=nbv: e.wait_ge(sV.h, v))
        P.em("T", lambda e, w=wv: e.wait_ge(d_set.h, w))



    pre_marks = {}
    pre_state = {}

    def prepass():
        pieces = []
        for gi, wsrc in enumerate((winT_d, wgT_d)):
            for k in range(NK):
                def mk(k=k, gi=gi):
                    def writes(hi, lo):
                        out = []
                        for toff, buf in ((2 * gi, hi), (2 * gi + 1, lo)):
                            dst = sg_d[:, :, k * 512 + toff * 128:
                                       k * 512 + toff * 128 + 128]
                            out.append((dst.rearrange("m p c -> p m c"),
                                        buf.rearrange("p (m c) -> p m c",
                                                      c=128)))
                        return out
                    return writes
                pieces.append((wsrc[k * 128:(k + 1) * 128, :], mk(), None))
        pieces[-1] = (pieces[-1][0], pieces[-1][1], "gates")
        for k in range(NK):
            for pc in range(6):
                c0 = pc * 2048
                w = min(2048, 2 * FHALF - c0)
                def mk(k=k, c0=c0, w=w):
                    def writes(hi, lo):
                        out = []
                        runs = []
                        for bi in range(w // 512):
                            g = c0 + bi * 512
                            fh, r = divmod(g, FHALF)
                            j = r // 512
                            if runs and runs[-1][0] == fh and                                     runs[-1][2] == j - (bi - runs[-1][1]):
                                pass
                            if runs and runs[-1][0] == fh:
                                runs[-1][3] += 1
                            else:
                                runs.append([fh, bi, j, 1])
                        for fh, bi0, j0, n in runs:
                            for which, buf in ((2 * fh, hi), (2 * fh + 1, lo)):
                                dst = s13_d[j0:j0 + n, k, :,
                                            which * 512:(which + 1) * 512]
                                out.append((
                                    dst.rearrange("j p c -> p j c"),
                                    buf[:, bi0 * 512:(bi0 + n) * 512]
                                    .rearrange("p (j c) -> p j c", c=512)))
                        return out
                    return writes
                pieces.append((w13T_d[k * 128:(k + 1) * 128, c0:c0 + w],
                               mk(), None))
        pieces[-1] = (pieces[-1][0], pieces[-1][1], "w13")
        for K in range(NKF):
            def mk(K=K):
                def writes(hi, lo):
                    out = []
                    for dh in range(2):
                        out.append((s2_d[dh, K, :, 0:1024],
                                    hi[:, dh * 1024:(dh + 1) * 1024]))
                        out.append((s2_d[dh, K, :, 1024:2048],
                                    lo[:, dh * 1024:(dh + 1) * 1024]))
                    return out
                return writes
            pieces.append((w2T_d[K * 128:(K + 1) * 128, :], mk(), None))
        pieces[-1] = (pieces[-1][0], pieces[-1][1], "w2")

        writes_after = []
        n_p = len(pieces)

        ld_eng = ("S", "G")

        def emit_load(i):
            slot = i % 2
            le = ld_eng[slot]
            srcap = pieces[i][0]
            w = srcap.shape[-1]
            pw = prew[slot][:, 0:w]
            if i >= 2:
                ps_, pv = writes_after[i - 2]
                P.em(le, lambda e, s=ps_, v=pv: e.wait_ge(s.h, v))
            P.em(le, lambda e, pw=pw, srcap=srcap, s=slot:
                 e.dma_start(out=pw, in_=srcap).then_inc(d_pl[s].h, 16))
            d_pl[slot].n += 16

        def emit_split_writes(i):
            slot = i % 2
            srcap, wfn, mark = pieces[i]
            w = srcap.shape[-1]
            pw = prew[slot][:, 0:w]
            phi = prehi[slot][:, 0:w]
            plo = prelo[slot][:, 0:w]
            lv = 16 * (i // 2 + 1)
            if i >= 2:
                ps_, pv = writes_after[i - 2]
                P.em("S", lambda e, s=ps_, v=pv: e.wait_ge(s.h, v))
                P.em("V", lambda e, s=ps_, v=pv: e.wait_ge(s.h, v))
            P.em("S", lambda e, v=lv, s=slot: e.wait_ge(d_pl[s].h, v))
            P.em("S", lambda e, phi=phi, pw=pw: e.activation(phi, pw, AF.Copy))
            hv = s_inc()
            P.em("V", lambda e, v=lv, s=slot: e.wait_ge(d_pl[s].h, v))
            P.em("V", lambda e, v=hv: e.wait_ge(sS.h, v))
            P.em("V", lambda e, plo=plo, pw=pw, phi=phi:
                 e.tensor_tensor(plo, pw, phi, ALU.subtract))
            lv2 = v_inc()
            dsts = wfn(phi, plo)
            if slot == 0:
                wq, wsem = "G", d_pw[0]
            else:
                wq, wsem = "S", d_pwq[1]
            P.em(wq, lambda e, v=hv: e.wait_ge(sS.h, v))
            P.em(wq, lambda e, v=lv2: e.wait_ge(sV.h, v))
            for dst, s3 in dsts:
                P.em(wq, lambda e, dst=dst, s3=s3, ws=wsem:
                     e.dma_start(out=dst, in_=s3).then_inc(ws.h, 16))
                wsem.n += 16
            writes_after.append((wsem, wsem.n))
            assert len(writes_after) == i + 1
            if mark:
                pre_marks[mark] = (d_pw[0].n, d_pw[1].n,
                                   d_pwq[0].n, d_pwq[1].n)

        emit_load(0)
        cursor = [0]

        def pre_pump(n):
            for _ in range(n):
                i = cursor[0]
                if i >= n_p:
                    return
                if i + 1 < n_p:
                    emit_load(i + 1)
                emit_split_writes(i)
                cursor[0] += 1

        pre_pump(72)
        pre_state["pump"] = pre_pump
        pre_state["left"] = lambda: n_p - cursor[0]

    def wait_pre(eng, mark):
        v0, v1, v2, v3 = pre_marks[mark]
        P.em(eng, lambda e, v=v0: e.wait_ge(d_pw[0].h, v))
        P.em(eng, lambda e, v=v1: e.wait_ge(d_pw[1].h, v))
        P.em(eng, lambda e, v=v2: e.wait_ge(d_pwq[0].h, v))
        P.em(eng, lambda e, v=v3: e.wait_ge(d_pwq[1].h, v))

    chain_done = [None]
    dr_done = [None, None]
    pend_tr = []

    def emit_tr(m, holdv):
        b_tr = 6 + (m % 2)
        t_wait_bank(b_tr)
        P.em("T", lambda e, v=holdv: e.wait_ge(sV.h, v))
        for ts in range(NT):
            P.em("T", lambda e, ts=ts, b=b_tr, hb=m % 2: e.matmul(
                pb[b][:, ts * 128:(ts + 1) * 128],
                bufDhold[hb][:, ts * 128:(ts + 1) * 128], id32,
                start=(ts == 0), stop=(ts == NT - 1), is_transpose=True))
        trv = t_inc()
        ydst = y_t.rearrange("p (tt d) -> p tt d", d=D)[
            :, :, m * 128:(m + 1) * 128]
        psrc = pb[b_tr].rearrange("p (tt cc) -> p tt cc", cc=128)
        P.em("V", lambda e, v=trv: e.wait_ge(sT.h, v))
        P.em("V", lambda e, ydst=ydst, psrc=psrc: e.tensor_copy(ydst, psrc))
        bank_free[b_tr] = (sV, v_inc())
        return trv
    r1_free = [None]

    p1_marks = {}

    def emit_P1(c, tguard=None):
        xdst = xstage.rearrange("p (tt d) -> p tt d", d=D)
        if r1_free[0] is not None:
            s, v = r1_free[0]
            P.em("G", lambda e, s=s, v=v: e.wait_ge(s.h, v))
        xsrc0 = x_d[c * T:c * T + 256, :].rearrange("(tt p) d -> p tt d", p=128)
        P.em("G", lambda e, xsrc0=xsrc0:
             e.dma_start(out=xdst, in_=xsrc0).then_inc(d_x.h, 16))
        d_x.n += 16
        xv = d_x.n

        if c == 0:
            for qe in ("Q", "S", "G"):
                wait_pre(qe, "gates")
        for m in range(NK):
            for q in range(4):
                src = sg_d[m, :, q * 2048:(q + 1) * 2048]
                def mkg(src=src):
                    def fn(eng, slot, ssem):
                        P.em(eng, lambda e, src=src, slot=slot, ss=ssem:
                             e.dma_start(out=gring[slot], in_=src)
                             .then_inc(ss.h, 16))
                    return fn
                gRing.push(mkg())
        gRing.pump()

        if tguard is not None:
            P.em("V", lambda e, v=tguard: e.wait_ge(sT.h, v))
        for h in range(2):
            P.em("T", lambda e, v=xv + 16 * h: e.wait_ge(d_x.h, v))
            for k in range(NK):
                b = 4 + (k % 2)
                t_wait_bank(b)
                for lt in range(2):
                    src = xstage[:, lt * D + k * 128: lt * D + k * 128 + 128]
                    P.em("T", lambda e, src=src, b=b, lt=lt: e.matmul(
                        pb[b][:, lt * 128:lt * 128 + 128], src, id32,
                        start=(lt == 0), stop=(lt == 1), is_transpose=True))
                tv = t_inc()
                P.em("V", lambda e, v=tv: e.wait_ge(sT.h, v))
                xts = xT_sl(k, h * 256, 256)
                P.em("V", lambda e, xts=xts, b=b: e.tensor_copy(
                    xts, pb[b][:, 0:256]))
                bank_free[b] = (sV, v_inc())
            if h == 0:
                hv = sV.n
                P.em("G", lambda e, v=hv: e.wait_ge(sV.h, v))
                xsrc1 = x_d[c * T + 256:(c + 1) * T, :].rearrange(
                    "(tt p) d -> p tt d", p=128)
                P.em("G", lambda e, xsrc1=xsrc1:
                     e.dma_start(out=xdst, in_=xsrc1).then_inc(d_x.h, 16))
                d_x.n += 16
        split_mark = []
        for k in range(NK):
            xts = xT_sl(k, 0, T)
            P.em("V", lambda e, k=k, xts=xts:
                 e.tensor_copy(xhi[:, k * T:(k + 1) * T], xts))
            P.em("V", lambda e, k=k, xts=xts: e.tensor_tensor(
                xlo[:, k * T:(k + 1) * T], xts, xhi[:, k * T:(k + 1) * T],
                ALU.subtract))
            split_mark.append(v_inc())
        p1_marks[c] = split_mark

    def do_chunk(c):
        if c == 0:
            emit_P1(0)
        split_mark = p1_marks.pop(c)

        if c == 0:
            wait_pre("T", "gates")
        if c > 0:
            P.em("V", lambda e, v=64 * c: e.wait_ge(d_out.h, v))
        for m in range(NK):
            b_ig = (m % 2) * 2
            b_rg = b_ig + 1
            t_wait_bank(b_ig)
            t_wait_bank(b_rg)
            cnt = [0, 0]
            nmm = 3 * NK
            for k in range(NK):
                q = k // 4
                if k % 4 == 0:
                    gRing.emit_wait("T", c * 64 + m * 4 + q)
                if m == 0:
                    P.em("T", lambda e, v=split_mark[k]: e.wait_ge(sV.h, v))
                slot = (c * 64 + m * 4 + q) % 3
                koff = (k % 4) * 512
                hi_in = gring[slot][:, koff:koff + 128]
                lo_in = gring[slot][:, koff + 128:koff + 256]
                hi_g = gring[slot][:, koff + 256:koff + 384]
                lo_g = gring[slot][:, koff + 384:koff + 512]
                rh = xhi[:, k * T:(k + 1) * T]
                rl = xlo[:, k * T:(k + 1) * T]
                for ci, (bk, whi, wlo) in enumerate(((b_ig, hi_in, lo_in),
                                                     (b_rg, hi_g, lo_g))):
                    for wt, rr in ((whi, rh), (whi, rl), (wlo, rh)):
                        st = (cnt[ci] == 0)
                        sp = (cnt[ci] == nmm - 1)
                        cnt[ci] += 1
                        P.em("T", lambda e, bk=bk, wt=wt, rr=rr, st=st, sp=sp:
                             e.matmul(pb[bk], wt, rr, start=st, stop=sp))
                if k % 4 == 3:
                    t_inc()
                    gRing.mark(sT)
            gdone = sT.n

            P.em("S", lambda e, v=gdone: e.wait_ge(sT.h, v))
            if chain_done[0] is not None:
                P.em("S", lambda e, v=chain_done[0]: e.wait_ge(sV.h, v))
            P.em("S", lambda e, m=m, b=b_ig: e.activation(
                bufA, pb[b], AF.Exp, bias=nbin[:, m:m + 1],
                scale=consts[:, 0:1]))
            P.em("S", lambda e, m=m, b=b_rg: e.activation(
                bufB, pb[b], AF.Exp, bias=nbg[:, m:m + 1],
                scale=consts[:, 0:1]))
            ev = s_inc()
            bank_free[b_ig] = (sS, ev)
            bank_free[b_rg] = (sS, ev)
            P.em("V", lambda e, v=ev: e.wait_ge(sS.h, v))
            P.em("V", lambda e: e.tensor_scalar(bufA, bufA, 1.0, None, ALU.add))
            P.em("V", lambda e: e.reciprocal(bufC, bufA))
            v_sync()
            P.em("V", lambda e: e.tensor_scalar(bufB, bufB, 1.0, None, ALU.add))
            P.em("V", lambda e: e.reciprocal(bufA, bufB))
            sigv = v_inc()
            P.em("V", lambda e, v=sigv: e.wait_ge(sV.h, v))
            P.em("S", lambda e, v=sigv: e.wait_ge(sV.h, v))
            P.em("S", lambda e, m=m: e.activation(
                bufB, bufA, AF.Exp, scale=a_e[:, m:m + 1]))
            av = s_inc()
            P.em("V", lambda e, v=av: e.wait_ge(sS.h, v))
            P.em("V", lambda e: e.tensor_tensor(bufD, bufB, bufB, ALU.mult))
            P.em("V", lambda e: e.tensor_scalar(bufD, bufD, -1.0, 1.0,
                                                ALU.mult, ALU.add))
            t2v = v_inc()
            P.em("S", lambda e, v=t2v: e.wait_ge(sV.h, v))
            P.em("S", lambda e: e.activation(bufD, bufD, AF.Ln,
                                             bias=consts[:, 3:4]))
            s_sync()
            P.em("S", lambda e: e.activation(bufD, bufD, AF.Exp,
                                             scale=consts[:, 1:2]))
            bv = s_inc()
            xts = xT_sl(m, 0, T)
            P.em("V", lambda e, v=bv: e.wait_ge(sS.h, v))
            P.em("V", lambda e, xts=xts: e.tensor_tensor(bufE, bufC, xts,
                                                         ALU.mult))
            P.em("V", lambda e: e.tensor_tensor(bufE, bufE, bufD, ALU.mult))
            v_sync()
            if c == 0:
                P.em("V", lambda e: e.tensor_tensor_scan(
                    bufC, bufB, bufE, 0.0, ALU.mult, ALU.add))
            else:
                P.em("V", lambda e, m=m: e.tensor_tensor_scan(
                    bufC, bufB, bufE, carry[:, m:m + 1], ALU.mult, ALU.add))
            v_sync()
            P.em("V", lambda e, m=m: e.tensor_copy(carry[:, m:m + 1],
                                                   bufC[:, T - 1:T]))
            v_sync()
            P.em("V", lambda e, xts=xts: e.tensor_tensor(bufD, bufC, xts,
                                                         ALU.add))
            r1v = v_inc()
            chain_done[0] = r1v
            pend_tr.append((m, r1v))
            if debug_dump and c == 0 and m == 0:
                P.em("G", lambda e, v=r1v: e.wait_ge(sV.h, v))
                for bi, bb in enumerate((bufA, bufB, bufC, bufD, bufE)):
                    P.em("G", lambda e, bb=bb, bi=bi: e.dma_start(
                        out=dbg["dbg_sig"][:, bi * T:(bi + 1) * T], in_=bb)
                        .then_inc(d_out.h, 16))
                    d_out.n += 16
            if len(pend_tr) > 2:
                tr_prev = emit_tr(*pend_tr.pop(0))
                P.em("V", lambda e, v=tr_prev: e.wait_ge(sT.h, v))
            P.em("V", lambda e, hb=m % 2: e.tensor_copy(bufDhold[hb], bufD))
            holdv = v_inc()
            pend_tr[-1] = (m, holdv)
            if c == 0:
                pre_state["pump"](9)
        while pend_tr:
            emit_tr(*pend_tr.pop(0))
        if c == 0:
            pre_state["pump"](10 ** 9)
        yv = sV.n
        if debug_dump and c == 0:
            P.em("G", lambda e, v=yv: e.wait_ge(sV.h, v))
            P.em("G", lambda e: e.dma_start(
                out=dbg["dbg_xT"][:, :8 * T], in_=xTa).then_inc(d_out.h, 16))
            P.em("G", lambda e: e.dma_start(
                out=dbg["dbg_xT"][:, 8 * T:], in_=xTb).then_inc(d_out.h, 16))
            d_out.n += 32
            P.em("V", lambda e, v=d_out.n: e.wait_ge(d_out.h, v))
            P.em("S", lambda e, v=d_out.n: e.wait_ge(d_out.h, v))
            P.em("T", lambda e, v=d_out.n: e.wait_ge(d_out.h, v))
            P.em("G", lambda e: e.dma_start(out=dbg["dbg_y0"], in_=y_t)
                 .then_inc(d_out.h, 16))
            d_out.n += 16
            P.em("V", lambda e, v=d_out.n: e.wait_ge(d_out.h, v))
            P.em("S", lambda e, v=d_out.n: e.wait_ge(d_out.h, v))
            P.em("T", lambda e, v=d_out.n: e.wait_ge(d_out.h, v))

        if c == 0:
            wait_pre("S", "w2")
            wait_pre("V", "w2")
        P.em("S", lambda e, v=yv: e.wait_ge(sV.h, v))
        for tt in range(NT):
            yt = y_t[:, tt * D:(tt + 1) * D]
            P.em("S", lambda e, yt=yt, tt=tt: e.activation(
                sqscr, yt, AF.Square, accum_out=ssq[:, tt:tt + 1]))
        sqv = s_inc()
        P.em("V", lambda e, v=yv: e.wait_ge(sV.h, v))
        for tt in range(NT):
            yt = y_t[:, tt * D:(tt + 1) * D]
            P.em("V", lambda e, yt=yt, tt=tt: e.reduce_sum(
                ssum[:, tt:tt + 1], yt, axis=AX.X))
        v_sync()
        P.em("V", lambda e, v=sqv: e.wait_ge(sS.h, v))
        P.em("V", lambda e: e.tensor_scalar(smean, ssum, 1.0 / D, None,
                                            ALU.mult))
        P.em("V", lambda e: e.tensor_scalar(svar, ssq, 1.0 / D, None,
                                            ALU.mult))
        v_sync()
        P.em("V", lambda e: e.tensor_tensor(stt1, smean, smean, ALU.mult))
        v_sync()
        P.em("V", lambda e: e.tensor_tensor(svar, svar, stt1, ALU.subtract))
        vv = v_inc()
        P.em("S", lambda e, v=vv: e.wait_ge(sV.h, v))
        P.em("S", lambda e: e.activation(sinv, svar, AF.Ln,
                                         bias=consts[:, 4:5]))
        s_sync()
        P.em("S", lambda e: e.activation(sinv, sinv, AF.Exp,
                                         scale=consts[:, 2:3]))
        iv = s_inc()
        P.em("V", lambda e, v=iv: e.wait_ge(sS.h, v))
        P.em("V", lambda e: e.tensor_tensor(snmi, smean, sinv, ALU.mult))
        v_sync()
        P.em("V", lambda e: e.tensor_scalar(snmi, snmi, -1.0, None, ALU.mult))
        nv = v_inc()
        P.em("S", lambda e, v=nv: e.wait_ge(sV.h, v))
        for tt in range(NT):
            yt = y_t[:, tt * D:(tt + 1) * D]
            P.em("S", lambda e, yt=yt, tt=tt: e.activation(
                yt, yt, AF.Identity, bias=snmi[:, tt:tt + 1],
                scale=sinv[:, tt:tt + 1]))
        ynorm = s_inc()
        if debug_dump and c == 0:
            P.em("G", lambda e, v=ynorm: e.wait_ge(sS.h, v))
            for i, stt_ in enumerate((ssum, ssq, smean, svar, sinv, snmi)):
                P.em("G", lambda e, stt_=stt_, i=i: e.dma_start(
                    out=dbg["dbg_sig"][:, 5 * T + i * 8:5 * T + i * 8 + NT],
                    in_=stt_).then_inc(d_out.h, 16))
                d_out.n += 16
            P.em("V", lambda e, v=d_out.n: e.wait_ge(d_out.h, v))
            P.em("S", lambda e, v=d_out.n: e.wait_ge(d_out.h, v))
        P.em("V", lambda e, v=ynorm: e.wait_ge(sS.h, v))
        for tt in range(NT):
            yt = y_t[:, tt * D:(tt + 1) * D]
            P.em("V", lambda e, yt=yt, tt=tt: e.reduce_max(
                sa1[:, tt:tt + 1], yt, axis=AX.X, apply_absolute_value=True))
        v_sync()
        P.em("V", lambda e: e.tensor_scalar_max(sa1, sa1, 1e-5))
        v_sync()
        P.em("V", lambda e: e.tensor_scalar(srs1, sa1, 1.0 / 127.0, None,
                                            ALU.mult))
        P.em("V", lambda e: e.reciprocal(ss1, sa1))
        v_sync()
        P.em("V", lambda e: e.tensor_scalar(ss1, ss1, 127.0, None, ALU.mult))
        v_sync()
        for tt in range(NT):
            yt = y_t[:, tt * D:(tt + 1) * D]
            P.em("V", lambda e, yt=yt, tt=tt: e.tensor_scalar(
                q1f16, yt, ss1[:, tt:tt + 1], MAGIC, ALU.mult, ALU.add))
            P.em("V", lambda e, tt=tt: e.tensor_scalar(
                q1tm[:, tt * D:(tt + 1) * D], q1f16, MAGIC, None,
                ALU.subtract))
        q1v = v_inc()
        P.em("T", lambda e, v=q1v: e.wait_ge(sV.h, v))
        q1f_mark = []
        for k in range(NK):
            b = 4 + (k % 2)
            t_wait_bank(b)
            for tt in range(NT):
                src = q1tm[:, tt * D + k * 128: tt * D + k * 128 + 128]
                P.em("T", lambda e, src=src, b=b, tt=tt: e.matmul(
                    pb[b][:, tt * 128:(tt + 1) * 128], src, id32,
                    start=(tt == 0), stop=(tt == NT - 1), is_transpose=True))
            tv = t_inc()
            P.em("V", lambda e, v=tv: e.wait_ge(sT.h, v))
            P.em("V", lambda e, k=k, b=b: e.tensor_copy(
                q1F[:, k * T:(k + 1) * T], pb[b]))
            q1f_mark.append(v_inc())
            bank_free[b] = (sV, q1f_mark[-1])
        q1fv = sV.n
        if debug_dump and c == 0:
            P.em("G", lambda e, v=q1fv: e.wait_ge(sV.h, v))
            P.em("G", lambda e: e.dma_start(out=dbg["dbg_q1F"],
                                            in_=q1F.astype(F32)
                                            if False else q1F)
                 .then_inc(d_out.h, 16))
            d_out.n += 16
            P.em("G", lambda e: e.dma_start(out=dbg["dbg_y1"], in_=y_t)
                 .then_inc(d_out.h, 16))
            d_out.n += 16
            P.em("V", lambda e, v=d_out.n: e.wait_ge(d_out.h, v))
            P.em("S", lambda e, v=d_out.n: e.wait_ge(d_out.h, v))
            P.em("T", lambda e, v=d_out.n: e.wait_ge(d_out.h, v))

        if c == 0:
            for qe in ("Q", "S", "G"):
                wait_pre(qe, "w13")
        for j in range(NJ):
            for k in range(NK):
                src = s13_d[j, k]
                def mkload(src=src):
                    def fn(eng, slot, ssem):
                        P.em(eng, lambda e, src=src, slot=slot, ss=ssem:
                             e.dma_start(out=wring[slot], in_=src)
                             .then_inc(ss.h, 16))
                    return fn
                wRing.push(mkload())
        wRing.pump()

        wseq0 = wRing.seq_marked
        for j in range(NJ):
            for b in range(8):
                t_wait_bank(b)
            first = [True] * 8
            for k in range(NK):
                seq = wseq0 + j * NK + k
                wRing.emit_wait("T", seq)
                if j == 0:
                    P.em("T", lambda e, v=q1f_mark[k]: e.wait_ge(sV.h, v))
                slot = seq % 3
                hi0 = wring[slot][:, 0:512]
                lo0 = wring[slot][:, 512:1024]
                hi1 = wring[slot][:, 1024:1536]
                lo1 = wring[slot][:, 1536:2048]
                for m in range(NT):
                    lhs = q1F[:, k * T + m * 128: k * T + m * 128 + 128]
                    for hb, whi, wlo in ((0, hi0, lo0), (1, hi1, lo1)):
                        bk = m * 2 + hb
                        st = first[bk]
                        first[bk] = False
                        sp = (k == NK - 1)
                        P.em("T", lambda e, bk=bk, lhs=lhs, w=whi, st=st:
                             e.matmul(pb[bk], lhs, w, start=st, stop=False))
                        P.em("T", lambda e, bk=bk, lhs=lhs, w=wlo, sp=sp:
                             e.matmul(pb[bk], lhs, w, start=False, stop=sp))
                t_inc()
                wRing.mark(sT)
            jv = sT.n
            P.em("S", lambda e, v=jv: e.wait_ge(sT.h, v))
            for m in range(NT):
                d0 = drbuf[0] if m % 2 == 0 else q2bf[0]
                d1 = drbuf[1] if m % 2 == 0 else q2bf[1]
                if dr_done[m % 2] is not None:
                    P.em("S", lambda e, v=dr_done[m % 2]: e.wait_ge(sV.h, v))
                P.em("S", lambda e, m=m, d0=d0: e.activation(
                    d0, pb[m * 2], AF.Relu, scale=srs1[:, m:m + 1]))
                P.em("S", lambda e, m=m, d1=d1: e.activation(
                    d1, pb[m * 2 + 1], AF.Identity, scale=srs1[:, m:m + 1]))
                dv = s_inc()
                bank_free[m * 2] = (sS, dv)
                bank_free[m * 2 + 1] = (sS, dv)
                gs = gnw[:, (j * 4 + m) * 512:(j * 4 + m + 1) * 512]
                P.em("V", lambda e, v=dv: e.wait_ge(sS.h, v))
                P.em("V", lambda e, g=gs, d0=d0: e.tensor_tensor(
                    g, d0, d0, ALU.mult))
                P.em("V", lambda e, g=gs, d1=d1: e.tensor_tensor(
                    g, g, d1, ALU.mult))
                P.em("V", lambda e, g=gs, d0=d0: e.tensor_tensor(
                    d0, g, g, ALU.mult))
                P.em("V", lambda e, m=m, j=j, d0=d0: e.reduce_sum(
                    sumb[:, m * NJ + j:m * NJ + j + 1], d0, axis=AX.X))
                P.em("V", lambda e, g=gs, j=j: e.tensor_tensor(
                    g, g, nwrep[:, j * 512:(j + 1) * 512], ALU.mult))
                P.em("V", lambda e, g=gs, m=m, j=j: e.reduce_max(
                    maxb[:, m * NJ + j:m * NJ + j + 1], g, axis=AX.X,
                    apply_absolute_value=True))
                dr_done[m % 2] = v_inc()

        v_sync()
        for m in range(NT):
            P.em("V", lambda e, m=m: e.reduce_sum(
                sms[:, m:m + 1], sumb[:, m * NJ:(m + 1) * NJ], axis=AX.X))
            P.em("V", lambda e, m=m: e.reduce_max(
                samax[:, m:m + 1], maxb[:, m * NJ:(m + 1) * NJ], axis=AX.X))
        v_sync()
        rdv = sV.n
        P.em("S", lambda e, v=rdv: e.wait_ge(sV.h, v))
        P.em("S", lambda e: e.activation(srstd, sms, AF.Ln,
                                         bias=consts[:, 4:5],
                                         scale=consts[:, 5:6]))
        s_sync()
        P.em("S", lambda e: e.activation(srstd, srstd, AF.Exp,
                                         scale=consts[:, 2:3]))
        rsv = s_inc()
        P.em("V", lambda e, v=rsv: e.wait_ge(sS.h, v))
        P.em("V", lambda e: e.tensor_tensor(stt1, samax, srstd, ALU.mult))
        v_sync()
        P.em("V", lambda e: e.tensor_scalar_max(stt1, stt1, 1e-5))
        v_sync()
        P.em("V", lambda e: e.tensor_scalar(srr2, stt1, 1.0 / 127.0, None,
                                            ALU.mult))
        P.em("V", lambda e: e.reciprocal(ss2c, stt1))
        v_sync()
        P.em("V", lambda e: e.tensor_scalar(ss2c, ss2c, 127.0, None, ALU.mult))
        v_sync()
        P.em("V", lambda e: e.tensor_tensor(srs2, srstd, ss2c, ALU.mult))
        v_sync()
        qtr = {}
        for j in range(NJ):
            for m in range(NT):
                idx = j * 4 + m
                gs = gnw[:, idx * 512:(idx + 1) * 512]
                if idx >= 4:
                    P.em("V", lambda e, v=qtr[idx - 4]: e.wait_ge(sT.h, v))
                P.em("V", lambda e, g=gs, m=m: e.tensor_scalar(
                    q2f16, g, srs2[:, m:m + 1], MAGIC, ALU.mult, ALU.add))
                if idx == 39:
                    r1_free[0] = (sV, v_inc())
                P.em("V", lambda e, idx=idx: e.tensor_scalar(
                    q2bf[idx % 4], q2f16, MAGIC, None, ALU.subtract))
                qv = v_inc()
                b = idx % 2
                t_wait_bank(b)
                P.em("T", lambda e, v=qv: e.wait_ge(sV.h, v))
                for s in range(4):
                    P.em("T", lambda e, s=s, b=b, idx=idx: e.matmul(
                        pb[b][:, s * 128:(s + 1) * 128],
                        q2bf[idx % 4][:, s * 128:(s + 1) * 128], id32,
                        start=(s == 0), stop=(s == 3), is_transpose=True))
                tv = t_inc()
                qtr[idx] = tv
                dst = iqT[:, idx * 512:(idx + 1) * 512]
                P.em("S", lambda e, v=tv: e.wait_ge(sT.h, v))
                P.em("S", lambda e, dst=dst, b=b: e.activation(
                    dst, pb[b], AF.Copy))
                bank_free[b] = (sS, s_inc())
        lastS = sS.n
        P.em("V", lambda e, v=lastS: e.wait_ge(sS.h, v))
        iqv = v_inc()

        if c + 1 < chunks:
            emit_P1(c + 1, tguard=qtr[NJ * 4 - 1])

        if c == 0:
            for qe in ("Q", "S", "G"):
                wait_pre(qe, "w2")
        for dh in range(2):
            for K in range(NKF):
                src = s2_d[dh, K]
                def mkload(src=src):
                    def fn(eng, slot, ssem):
                        P.em(eng, lambda e, src=src, slot=slot, ss=ssem:
                             e.dma_start(out=wring[slot], in_=src)
                             .then_inc(ss.h, 16))
                    return fn
                wRing.push(mkload())
        wRing.pump()

        P.em("T", lambda e, v=iqv: e.wait_ge(sV.h, v))
        wseq1 = wRing.seq_marked
        for dh in range(2):
            for b in range(8):
                t_wait_bank(b)
            for K in range(NKF):
                seq = wseq1 + dh * NKF + K
                wRing.emit_wait("T", seq)
                slot = seq % 3
                whis = [wring[slot][:, b2 * 512:(b2 + 1) * 512]
                        for b2 in range(2)]
                wlos = [wring[slot][:, 1024 + b2 * 512:1024 + (b2 + 1) * 512]
                        for b2 in range(2)]
                for m in range(NT):
                    koff = ((K // 4) * 4 + m) * 512 + (K % 4) * 128
                    lhs = iqT[:, koff:koff + 128]
                    for b2 in range(2):
                        bk = m * 2 + b2
                        P.em("T", lambda e, bk=bk, lhs=lhs, w=whis[b2],
                             st=(K == 0): e.matmul(pb[bk], lhs, w, start=st,
                                                   stop=False))
                        P.em("T", lambda e, bk=bk, lhs=lhs, w=wlos[b2],
                             sp=(K == NKF - 1): e.matmul(pb[bk], lhs, w,
                                                         start=False, stop=sp))
                t_inc()
                wRing.mark(sT)
            fv = sT.n
            P.em("V", lambda e, v=fv: e.wait_ge(sT.h, v))
            for m in range(NT):
                for b2 in range(2):
                    bk = m * 2 + b2
                    ydst = y_t[:, m * D + dh * 1024 + b2 * 512:
                               m * D + dh * 1024 + (b2 + 1) * 512]
                    P.em("V", lambda e, bk=bk, ydst=ydst, m=m:
                         e.scalar_tensor_tensor(ydst, pb[bk],
                                                srr2[:, m:m + 1], ydst,
                                                ALU.mult, ALU.add))
                    bank_free[bk] = (sV, v_inc())
        v_sync()
        r2v = sV.n

        if c == 0:
            wait_pre("S", "w2")
        P.em("S", lambda e, v=r2v: e.wait_ge(sV.h, v))
        for tt in range(NT):
            yt = y_t[:, tt * D:(tt + 1) * D]
            P.em("S", lambda e, yt=yt, tt=tt: e.activation(
                sqscr2, yt, AF.Square, accum_out=ssq[:, tt:tt + 1]))
        sqv2 = s_inc()
        P.em("V", lambda e, v=r2v: e.wait_ge(sV.h, v))
        for tt in range(NT):
            yt = y_t[:, tt * D:(tt + 1) * D]
            P.em("V", lambda e, yt=yt, tt=tt: e.reduce_sum(
                ssum[:, tt:tt + 1], yt, axis=AX.X))
        v_sync()
        P.em("V", lambda e, v=sqv2: e.wait_ge(sS.h, v))
        P.em("V", lambda e: e.tensor_scalar(smean, ssum, 1.0 / D, None,
                                            ALU.mult))
        P.em("V", lambda e: e.tensor_scalar(svar, ssq, 1.0 / D, None,
                                            ALU.mult))
        v_sync()
        P.em("V", lambda e: e.tensor_tensor(stt1, smean, smean, ALU.mult))
        v_sync()
        P.em("V", lambda e: e.tensor_tensor(svar, svar, stt1, ALU.subtract))
        vv2 = v_inc()
        P.em("S", lambda e, v=vv2: e.wait_ge(sV.h, v))
        P.em("S", lambda e: e.activation(sinv, svar, AF.Ln,
                                         bias=consts[:, 4:5]))
        s_sync()
        P.em("S", lambda e: e.activation(sinv, sinv, AF.Exp,
                                         scale=consts[:, 2:3]))
        iv2 = s_inc()
        P.em("V", lambda e, v=iv2: e.wait_ge(sS.h, v))
        P.em("V", lambda e: e.tensor_tensor(snmi, smean, sinv, ALU.mult))
        v_sync()
        P.em("V", lambda e: e.tensor_scalar(snmi, snmi, -1.0, None, ALU.mult))
        nv2 = v_inc()
        P.em("S", lambda e, v=nv2: e.wait_ge(sV.h, v))
        for tt in range(NT):
            yt = y_t[:, tt * D:(tt + 1) * D]
            P.em("S", lambda e, yt=yt, tt=tt: e.activation(
                yt, yt, AF.Identity, bias=snmi[:, tt:tt + 1],
                scale=sinv[:, tt:tt + 1]))
            ov = s_inc()
            odst = out_d[c * T + tt * 128:c * T + (tt + 1) * 128, :]
            P.em("G", lambda e, v=ov: e.wait_ge(sS.h, v))
            P.em("G", lambda e, odst=odst, yt=yt:
                 e.dma_start(out=odst, in_=yt).then_inc(d_out.h, 16))
            d_out.n += 16

    setup()
    prepass()
    for c in range(chunks):
        do_chunk(c)
    P.em("G", lambda e, v=d_out.n: e.wait_ge(d_out.h, v))
    P.do_splice()

    with nc.Block() as block:
        @block.tensor
        def _(eng):
            for fn in P.streams["T"]:
                fn(eng)

        @block.vector
        def _(eng):
            for fn in P.streams["V"]:
                fn(eng)

        @block.scalar
        def _(eng):
            for fn in P.streams["S"]:
                fn(eng)

        @block.sync
        def _(eng):
            for fn in P.streams["Q"]:
                fn(eng)

        @block.gpsimd
        def _(eng):
            for fn in P.streams["G"]:
                fn(eng)

    return nc


_NC_CACHE = None


def kernel(L0_attn_x, L0_attn_W_in, L0_attn_b_in, L0_attn_W_gate,
           L0_attn_b_gate, L0_attn_forget_lambda, L0_ffn_w13, L0_ffn_w2,
           L0_ffn_norm_w):
    global _NC_CACHE
    import ml_dtypes
    x = np.ascontiguousarray(np.asarray(L0_attn_x, dtype=np.float32))
    winT = np.ascontiguousarray(np.asarray(L0_attn_W_in, np.float32).T)
    wgT = np.ascontiguousarray(np.asarray(L0_attn_W_gate, np.float32).T)
    w13T = np.ascontiguousarray(np.asarray(L0_ffn_w13, np.float32).T)
    w2T = np.ascontiguousarray(np.asarray(L0_ffn_w2, np.float32).T)
    bin_c = np.ascontiguousarray(
        np.asarray(L0_attn_b_in, np.float32).reshape(NK, 128).T)
    bg_c = np.ascontiguousarray(
        np.asarray(L0_attn_b_gate, np.float32).reshape(NK, 128).T)
    lam_c = np.ascontiguousarray(
        np.asarray(L0_attn_forget_lambda, np.float32).reshape(NK, 128).T)
    nw_r = np.ascontiguousarray(np.broadcast_to(
        np.asarray(L0_ffn_norm_w, np.float32).reshape(1, FHALF),
        (128, FHALF)))
    id32 = np.eye(128, dtype=np.float32)
    id16 = np.eye(128, dtype=ml_dtypes.bfloat16)

    if _NC_CACHE is None:
        _NC_CACHE = build_program()
    nc = _NC_CACHE

    shared = dict(w_inT=winT, w_gT=wgT, w13T=w13T, w2T=w2T, b_in_col=bin_c,
                  b_g_col=bg_c, lam_col=lam_c, normw_rep_in=nw_r, ident32=id32,
                  ident16=id16)
    in_maps = [dict(shared, x=np.ascontiguousarray(x[b])) for b in range(8)]
    res = run_bass_kernel_spmd(nc, in_maps, core_ids=list(range(8)))
    return np.stack([r["out"] for r in res.results], axis=0)


# version 18 (final)
# speedup vs baseline: 1.0578x; 1.0578x over previous
import numpy as np
import concourse.bass as bass
import concourse.mybir as mybir
from concourse.bass_utils import run_bass_kernel_spmd

F32 = mybir.dt.float32
BF16 = mybir.dt.bfloat16
F16 = mybir.dt.float16
AF = mybir.ActivationFunctionType
ALU = mybir.AluOpType
AX = mybir.AxisListType

D = 2048
TFULL = 4096
CH = 8
T = 512
NT = T // 128
NK = D // 128
FHALF = 5632
NJ = FHALF // 512
NKF = FHALF // 128
MAGIC = 1536.0


class Sem:
    def __init__(self, nc, name):
        self.h = nc.alloc_semaphore(name)
        self.n = 0


class Prog:
    def __init__(self):
        self.streams = {k: [] for k in "TVSQG"}
        self.side = []
        self.splice_idx = None

    def em(self, eng, fn):
        if eng == "QPRE":
            self.side.append(fn)
        else:
            self.streams[eng].append(fn)

    def mark_splice(self):
        self.splice_idx = len(self.streams["Q"])

    def do_splice(self):
        i = self.splice_idx if self.splice_idx is not None else 0
        self.streams["Q"] = (self.streams["Q"][:i] + self.side +
                             self.streams["Q"][i:])
        self.side = []


class Ring:

    SLOTS = 3

    def __init__(self, P, engs, nc, name):
        self.P = P
        self.engs = engs if isinstance(engs, (list, tuple)) \
            else [engs] * self.SLOTS
        self.sems = [Sem(nc, f"{name}_s{i}") for i in range(self.SLOTS)]
        self.queue = []
        self.seq_load = 0
        self.seq_marked = 0
        self.marks = {}
        self.slots = self.SLOTS

    def push(self, fn):
        self.queue.append(fn)

    def pump(self):
        while self.queue and self.seq_load < self.seq_marked + self.slots:
            fn = self.queue.pop(0)
            seq = self.seq_load
            slot = seq % self.slots
            prior = self.marks.pop(seq - self.slots, None)
            if prior is not None:
                s, c = prior
                self.P.em(self.engs[slot],
                          lambda e, s=s, c=c: e.wait_ge(s.h, c))
            fn(self.engs[slot], slot, self.sems[slot])
            self.sems[slot].n += 16
            self.seq_load += 1

    def mark(self, sem):
        self.marks[self.seq_marked] = (sem, sem.n)
        self.seq_marked += 1
        self.pump()

    def emit_wait(self, eng, seq):
        sem = self.sems[seq % self.slots]
        val = 16 * (seq // self.slots + 1)
        self.P.em(eng, lambda e, s=sem, v=val: e.wait_ge(s.h, v))


def build_program(debug_dump=False, chunks=CH):
    nc = bass.Bass("TRN2", target_bir_lowering=False, debug=False,
                   detect_race_conditions=False)

    x_d = nc.dram_tensor("x", [TFULL, D], F32, kind="ExternalInput").ap()
    wgp_d = nc.dram_tensor("wg_proto", [NK, 4, 128, 1024], F32,
                           kind="ExternalInput").ap()
    w13T_d = nc.dram_tensor("w13T", [D, 2 * FHALF], F32, kind="ExternalInput").ap()
    w2T_d = nc.dram_tensor("w2T", [FHALF, D], F32, kind="ExternalInput").ap()
    bin_d = nc.dram_tensor("b_in_col", [128, NK], F32, kind="ExternalInput").ap()
    bg_d = nc.dram_tensor("b_g_col", [128, NK], F32, kind="ExternalInput").ap()
    lam_d = nc.dram_tensor("lam_col", [128, NK], F32, kind="ExternalInput").ap()
    nw_d = nc.dram_tensor("normw_rep_in", [128, FHALF], F32, kind="ExternalInput").ap()
    id32_d = nc.dram_tensor("ident32", [128, 128], F32, kind="ExternalInput").ap()
    id16_d = nc.dram_tensor("ident16", [128, 128], BF16, kind="ExternalInput").ap()
    out_d = nc.dram_tensor("out", [TFULL, D], F32, kind="ExternalOutput").ap()
    dbg = {}
    if debug_dump:
        for nm, shp in (("dbg_xT", [128, NK * T]), ("dbg_y1", [128, NT * D]),
                        ("dbg_q1F", [128, NK * T]), ("dbg_gnw", [128, NJ * 4 * 512]),
                        ("dbg_iqT", [128, NKF * 512]), ("dbg_r2", [128, NT * D]),
                        ("dbg_sig", [128, 10 * T]),
                        ("dbg_y0", [128, NT * D])):
            dbg[nm] = nc.dram_tensor(nm, shp, F32, kind="ExternalOutput").ap()

    sg_d = nc.dram_tensor("s_gates", [NK, 128, NK * 4 * 128], BF16).ap()
    s13_d = nc.dram_tensor("s_w13", [NJ, NK, 128, 4 * 512], BF16).ap()
    s2_d = nc.dram_tensor("s_w2", [2, NKF, 128, 2048], BF16).ap()

    def pin(name, shape, dtype, off):
        return nc.alloc_sbuf_tensor_at(name, shape, dtype, offset=off).ap()

    BASE = 17408
    xstage = pin("xstage", [128, 2 * D], F32, BASE + 65536)
    sqscr = pin("sqscr", [128, D], F32, BASE)
    q1f16 = pin("q1f16", [128, D], F16, BASE + 8192)
    q1tm = pin("q1tm", [128, NT * D], F32, BASE + 16384)
    xTa = pin("xTa", [128, 8 * T], F32, BASE + 49152)
    xTb = pin("xTb", [128, 8 * T], F32, BASE + 131072)
    xhi = pin("xhi", [128, NK * T], BF16, BASE + 65536)
    xlo = pin("xlo", [128, NK * T], BF16, BASE + 81920)
    gnw = pin("gnw", [128, NJ * 4 * 512], F32, BASE)
    iqT = pin("iqT", [128, NKF * 512], BF16, BASE)
    sqscr2 = pin("sqscr2", [128, D], F32, BASE + 16384)
    drbuf = [pin(f"dr{i}", [128, 512], F32, BASE + 90112 + i * 2048)
             for i in range(2)]
    q2bf = [pin(f"q2bf{i}", [128, 512], F32,
                BASE + (94208 + i * 2048 if i < 2 else 90112 + (i - 2) * 2048))
            for i in range(4)]
    prew = [pin(f"prew{i}", [128, D], F32, BASE + i * 8192) for i in range(2)]
    prehi = [pin(f"prehi{i}", [128, D], BF16, BASE + 16384 + i * 4096)
             for i in range(2)]
    prelo = [pin(f"prelo{i}", [128, D], BF16, BASE + 24576 + i * 4096)
             for i in range(2)]

    def xT_sl(k, o0, ln):
        t_ = xTa if k < 8 else xTb
        return t_[:, (k % 8) * T + o0:(k % 8) * T + o0 + ln]

    y_t = pin("y", [128, NT * D], F32, BASE + 98304)
    q1F = pin("q1F", [128, NK * T], BF16, BASE + 131072)
    gring = [pin(f"gring{i}", [128, 2048], BF16, BASE + 147456 + i * 4096)
             for i in range(3)]
    wring = [pin(f"wring{i}", [128, 2048], BF16, BASE + 159744 + i * 4096)
             for i in range(3)]
    nwrep = pin("nwrep", [128, FHALF], F32, BASE + 172032)

    sb = nc.alloc_sbuf_tensor

    def right(name, shape, dtype):
        return sb(name, shape, dtype, side="right").ap()

    id32 = right("id32", [128, 128], F32)
    id16 = right("id16", [128, 128], BF16)
    a_e = right("a_e", [128, NK], F32)
    nbin = right("nbin", [128, NK], F32)
    nbg = right("nbg", [128, NK], F32)
    lamc = right("lamc", [128, NK], F32)
    carry = right("carry", [128, NK], F32)
    consts = right("consts", [128, 8], F32)
    ssum = right("ssum", [128, NT], F32)
    ssq = right("ssq", [128, NT], F32)
    smean = right("smean", [128, NT], F32)
    svar = right("svar", [128, NT], F32)
    sinv = right("sinv", [128, NT], F32)
    snmi = right("snmi", [128, NT], F32)
    sa1 = right("sa1", [128, NT], F32)
    ss1 = right("ss1", [128, NT], F32)
    srs1 = right("srs1", [128, NT], F32)
    sms = right("sms", [128, NT], F32)
    samax = right("samax", [128, NT], F32)
    srstd = right("srstd", [128, NT], F32)
    ss2c = right("ss2c", [128, NT], F32)
    srs2 = right("srs2", [128, NT], F32)
    srr2 = right("srr2", [128, NT], F32)
    stt1 = right("stt1", [128, NT], F32)
    sumb = right("sumb", [128, NKF], F32)
    maxb = right("maxb", [128, NKF], F32)
    bufA = right("bufA", [128, T], F32)
    bufB = right("bufB", [128, T], F32)
    bufC = right("bufC", [128, T], F32)
    bufD = right("bufD", [128, T], F32)
    bufE = right("bufE", [128, T], F32)
    bufDhold = [right(f"bufDh{i}", [128, T], F32) for i in range(2)]
    q2f16 = right("q2f16", [128, 512], F16)

    pb = [nc.alloc_psum_tensor(f"pb{i}", [128, 512], F32).ap() for i in range(8)]

    d_set = Sem(nc, "d_set")
    d_x = Sem(nc, "d_x")
    d_out = Sem(nc, "d_out")
    d_pl = [Sem(nc, f"d_pl{i}") for i in range(2)]
    d_pw = [Sem(nc, f"d_pw{i}") for i in range(2)]
    d_pwq = [Sem(nc, f"d_pwq{i}") for i in range(2)]
    sT = Sem(nc, "sT")
    sV = Sem(nc, "sV")
    sS = Sem(nc, "sS")

    P = Prog()
    gRing = Ring(P, ["Q", "Q", "Q"], nc, "gs")
    wRing = Ring(P, ["Q", "Q", "Q"], nc, "ws")

    bank_free = [None] * 8

    def t_wait_bank(b):
        bf = bank_free[b]
        if bf is not None:
            s, c = bf
            P.em("T", lambda e, s=s, c=c: e.wait_ge(s.h, c))

    inc_mark = {}

    def _inc(eng, sem):
        key = (eng, len(P.streams[eng]) - 1, sem)
        if inc_mark.get((eng, len(P.streams[eng]) - 1)) is sem:
            return sem.n
        sem.n += 1
        inc_mark[(eng, len(P.streams[eng]) - 1)] = sem
        prev = P.streams[eng][-1]

        def wrapped(e, prev=prev, s=sem):
            return prev(e).then_inc(s.h, 1)

        P.streams[eng][-1] = wrapped
        return sem.n

    def v_inc():
        return _inc("V", sV)

    def s_inc():
        return _inc("S", sS)

    def t_inc():
        return _inc("T", sT)

    def v_sync():
        v = v_inc()
        P.em("V", lambda e, v=v: e.wait_ge(sV.h, v))

    def s_sync():
        v = s_inc()
        P.em("S", lambda e, v=v: e.wait_ge(sS.h, v))

    def setup():
        def vsetup(e):
            e.memset(consts[:, 0:1], -1.0)
            e.memset(consts[:, 1:2], 0.5)
            e.memset(consts[:, 2:3], -0.5)
            e.memset(consts[:, 3:4], 1e-6)
            e.memset(consts[:, 4:5], 1e-5)
            e.memset(consts[:, 5:6], 1.0 / FHALF)
            e.memset(consts[:, 6:7], -8.0)
            e.memset(consts[:, 7:8], 1.0)
            return e.memset(carry, 0.0)
        P.em("V", vsetup)
        msv = v_inc()

        def gsetup(e):
            e.dma_start(out=id32, in_=id32_d).then_inc(d_set.h, 16)
            e.dma_start(out=id16, in_=id16_d).then_inc(d_set.h, 16)
            e.dma_start(out=nbin, in_=bin_d).then_inc(d_set.h, 16)
            e.dma_start(out=nbg, in_=bg_d).then_inc(d_set.h, 16)
            e.dma_start(out=lamc, in_=lam_d).then_inc(d_set.h, 16)
            return e.dma_start(out=nwrep, in_=nw_d).then_inc(d_set.h, 16)
        P.em("G", gsetup)
        d_set.n += 96
        wv = d_set.n

        P.em("V", lambda e, w=wv: e.wait_ge(d_set.h, w))
        P.em("V", lambda e: e.tensor_scalar(nbin, nbin, -1.0, None, ALU.mult))
        P.em("V", lambda e: e.tensor_scalar(nbg, nbg, -1.0, None, ALU.mult))
        nbv = v_inc()
        P.em("S", lambda e, w=wv: e.wait_ge(d_set.h, w))
        P.em("S", lambda e, v=msv: e.wait_ge(sV.h, v))
        P.em("S", lambda e: e.activation(a_e, lamc, AF.Exp))
        s_sync()
        P.em("S", lambda e: e.activation(a_e, a_e, AF.Ln,
                                         bias=consts[:, 7:8]))
        sfv = s_inc()
        P.em("V", lambda e, w=sfv: e.wait_ge(sS.h, w))
        P.em("V", lambda e: e.tensor_scalar(a_e, a_e, -8.0, None, ALU.mult))
        aev = v_inc()
        P.em("S", lambda e, v=aev: e.wait_ge(sV.h, v))
        P.em("S", lambda e, v=nbv: e.wait_ge(sV.h, v))
        P.em("T", lambda e, w=wv: e.wait_ge(d_set.h, w))



    pre_marks = {}
    pre_state = {}

    def prepass():
        pieces = []
        for m in range(NK):
            for q in range(4):
                def mkg(m=m, q=q):
                    def writes(hi, lo):
                        return [(sg_d[m, :, q * 2048:(q + 1) * 2048], hi)]
                    return writes
                pieces.append((wgp_d[m, q], mkg(), None, "g"))
        pieces[-1] = pieces[-1][:2] + ("gates", "g")
        for k in range(NK):
            for pc in range(6):
                c0 = pc * 2048
                w = min(2048, 2 * FHALF - c0)
                def mk(k=k, c0=c0, w=w):
                    def writes(hi, lo):
                        out = []
                        runs = []
                        for bi in range(w // 512):
                            g = c0 + bi * 512
                            fh, r = divmod(g, FHALF)
                            j = r // 512
                            if runs and runs[-1][0] == fh and                                     runs[-1][2] == j - (bi - runs[-1][1]):
                                pass
                            if runs and runs[-1][0] == fh:
                                runs[-1][3] += 1
                            else:
                                runs.append([fh, bi, j, 1])
                        for fh, bi0, j0, n in runs:
                            for which, buf in ((2 * fh, hi), (2 * fh + 1, lo)):
                                dst = s13_d[j0:j0 + n, k, :,
                                            which * 512:(which + 1) * 512]
                                out.append((
                                    dst.rearrange("j p c -> p j c"),
                                    buf[:, bi0 * 512:(bi0 + n) * 512]
                                    .rearrange("p (j c) -> p j c", c=512)))
                        return out
                    return writes
                pieces.append((w13T_d[k * 128:(k + 1) * 128, c0:c0 + w],
                               mk(), None))
        pieces[-1] = (pieces[-1][0], pieces[-1][1], "w13")
        for K in range(NKF):
            def mk(K=K):
                def writes(hi, lo):
                    out = []
                    for dh in range(2):
                        out.append((s2_d[dh, K, :, 0:1024],
                                    hi[:, dh * 1024:(dh + 1) * 1024]))
                        out.append((s2_d[dh, K, :, 1024:2048],
                                    lo[:, dh * 1024:(dh + 1) * 1024]))
                    return out
                return writes
            pieces.append((w2T_d[K * 128:(K + 1) * 128, :], mk(), None))
        pieces[-1] = (pieces[-1][0], pieces[-1][1], "w2")

        writes_after = []
        n_p = len(pieces)

        ld_eng = ("S", "G")

        def emit_load(i):
            slot = i % 2
            le = ld_eng[slot]
            srcap = pieces[i][0]
            w = srcap.shape[-1]
            pw = prew[slot][:, 0:w]
            if i >= 2:
                ps_, pv = writes_after[i - 2]
                P.em(le, lambda e, s=ps_, v=pv: e.wait_ge(s.h, v))
            P.em(le, lambda e, pw=pw, srcap=srcap, s=slot:
                 e.dma_start(out=pw, in_=srcap).then_inc(d_pl[s].h, 16))
            d_pl[slot].n += 16

        def emit_split_writes(i):
            slot = i % 2
            entry = pieces[i]
            srcap, wfn, mark = entry[0], entry[1], entry[2]
            kind = entry[3] if len(entry) > 3 else "w"
            w = srcap.shape[-1]
            pw = prew[slot][:, 0:w]
            phi = prehi[slot][:, 0:w]
            plo = prelo[slot][:, 0:w]
            lv = 16 * (i // 2 + 1)
            if i >= 2:
                ps_, pv = writes_after[i - 2]
                P.em("S", lambda e, s=ps_, v=pv: e.wait_ge(s.h, v))
                P.em("V", lambda e, s=ps_, v=pv: e.wait_ge(s.h, v))
            P.em("S", lambda e, v=lv, s=slot: e.wait_ge(d_pl[s].h, v))
            if kind == "g":
                slab4 = prehi[slot].rearrange("p (kk t mc) -> p kk t mc",
                                              t=4, mc=128)
                def pin4(gi):
                    return prew[slot][:, gi * 512:(gi + 1) * 512].rearrange(
                        "p (kk t mc) -> p kk t mc", t=1, mc=128)
                for gi in range(2):
                    P.em("S", lambda e, o=slab4[:, :, 2 * gi:2 * gi + 1, :],
                         s4=pin4(gi): e.activation(o, s4, AF.Copy))
                hv = s_inc()
                P.em("V", lambda e, v=lv, s=slot: e.wait_ge(d_pl[s].h, v))
                P.em("V", lambda e, v=hv: e.wait_ge(sS.h, v))
                for gi in range(2):
                    P.em("V", lambda e,
                         o=slab4[:, :, 2 * gi + 1:2 * gi + 2, :],
                         s4=pin4(gi), h4=slab4[:, :, 2 * gi:2 * gi + 1, :]:
                         e.tensor_tensor(o, s4, h4, ALU.subtract))
                lv2 = v_inc()
                dsts = wfn(prehi[slot], None)
            else:
                P.em("S", lambda e, phi=phi, pw=pw:
                     e.activation(phi, pw, AF.Copy))
                hv = s_inc()
                P.em("V", lambda e, v=lv, s=slot: e.wait_ge(d_pl[s].h, v))
                P.em("V", lambda e, v=hv: e.wait_ge(sS.h, v))
                P.em("V", lambda e, plo=plo, pw=pw, phi=phi:
                     e.tensor_tensor(plo, pw, phi, ALU.subtract))
                lv2 = v_inc()
                dsts = wfn(phi, plo)
            if slot == 0:
                wq, wsem = "G", d_pw[0]
            else:
                wq, wsem = "S", d_pwq[1]
            P.em(wq, lambda e, v=hv: e.wait_ge(sS.h, v))
            P.em(wq, lambda e, v=lv2: e.wait_ge(sV.h, v))
            for dst, s3 in dsts:
                P.em(wq, lambda e, dst=dst, s3=s3, ws=wsem:
                     e.dma_start(out=dst, in_=s3).then_inc(ws.h, 16))
                wsem.n += 16
            writes_after.append((wsem, wsem.n))
            assert len(writes_after) == i + 1
            if mark:
                pre_marks[mark] = (d_pw[0].n, d_pw[1].n,
                                   d_pwq[0].n, d_pwq[1].n)

        emit_load(0)
        cursor = [0]

        def pre_pump(n):
            for _ in range(n):
                i = cursor[0]
                if i >= n_p:
                    return
                if i + 1 < n_p:
                    emit_load(i + 1)
                emit_split_writes(i)
                cursor[0] += 1

        pre_pump(72)
        pre_state["pump"] = pre_pump
        pre_state["left"] = lambda: n_p - cursor[0]

    def wait_pre(eng, mark):
        v0, v1, v2, v3 = pre_marks[mark]
        P.em(eng, lambda e, v=v0: e.wait_ge(d_pw[0].h, v))
        P.em(eng, lambda e, v=v1: e.wait_ge(d_pw[1].h, v))
        P.em(eng, lambda e, v=v2: e.wait_ge(d_pwq[0].h, v))
        P.em(eng, lambda e, v=v3: e.wait_ge(d_pwq[1].h, v))

    chain_done = [None]
    dr_done = [None, None]
    pend_tr = []

    def emit_tr(m, holdv):
        b_tr = 6 + (m % 2)
        t_wait_bank(b_tr)
        P.em("T", lambda e, v=holdv: e.wait_ge(sV.h, v))
        for ts in range(NT):
            P.em("T", lambda e, ts=ts, b=b_tr, hb=m % 2: e.matmul(
                pb[b][:, ts * 128:(ts + 1) * 128],
                bufDhold[hb][:, ts * 128:(ts + 1) * 128], id32,
                start=(ts == 0), stop=(ts == NT - 1), is_transpose=True))
        trv = t_inc()
        ydst = y_t.rearrange("p (tt d) -> p tt d", d=D)[
            :, :, m * 128:(m + 1) * 128]
        psrc = pb[b_tr].rearrange("p (tt cc) -> p tt cc", cc=128)
        P.em("V", lambda e, v=trv: e.wait_ge(sT.h, v))
        P.em("V", lambda e, ydst=ydst, psrc=psrc: e.tensor_copy(ydst, psrc))
        bank_free[b_tr] = (sV, v_inc())
        return trv
    r1_free = [None]

    p1_marks = {}

    def emit_P1(c, tguard=None):
        xdst = xstage.rearrange("p (tt d) -> p tt d", d=D)
        if r1_free[0] is not None:
            s, v = r1_free[0]
            P.em("G", lambda e, s=s, v=v: e.wait_ge(s.h, v))
        xsrc0 = x_d[c * T:c * T + 256, :].rearrange("(tt p) d -> p tt d", p=128)
        P.em("G", lambda e, xsrc0=xsrc0:
             e.dma_start(out=xdst, in_=xsrc0).then_inc(d_x.h, 16))
        d_x.n += 16
        xv = d_x.n

        if c == 0:
            for qe in ("Q", "S", "G"):
                wait_pre(qe, "gates")
        for m in range(NK):
            for q in range(4):
                src = sg_d[m, :, q * 2048:(q + 1) * 2048]
                def mkg(src=src):
                    def fn(eng, slot, ssem):
                        P.em(eng, lambda e, src=src, slot=slot, ss=ssem:
                             e.dma_start(out=gring[slot], in_=src)
                             .then_inc(ss.h, 16))
                    return fn
                gRing.push(mkg())
        gRing.pump()

        if tguard is not None:
            P.em("V", lambda e, v=tguard: e.wait_ge(sT.h, v))
        for h in range(2):
            P.em("T", lambda e, v=xv + 16 * h: e.wait_ge(d_x.h, v))
            for k in range(NK):
                b = 4 + (k % 2)
                t_wait_bank(b)
                for lt in range(2):
                    src = xstage[:, lt * D + k * 128: lt * D + k * 128 + 128]
                    P.em("T", lambda e, src=src, b=b, lt=lt: e.matmul(
                        pb[b][:, lt * 128:lt * 128 + 128], src, id32,
                        start=(lt == 0), stop=(lt == 1), is_transpose=True))
                tv = t_inc()
                P.em("V", lambda e, v=tv: e.wait_ge(sT.h, v))
                xts = xT_sl(k, h * 256, 256)
                P.em("V", lambda e, xts=xts, b=b: e.tensor_copy(
                    xts, pb[b][:, 0:256]))
                bank_free[b] = (sV, v_inc())
            if h == 0:
                hv = sV.n
                P.em("G", lambda e, v=hv: e.wait_ge(sV.h, v))
                xsrc1 = x_d[c * T + 256:(c + 1) * T, :].rearrange(
                    "(tt p) d -> p tt d", p=128)
                P.em("G", lambda e, xsrc1=xsrc1:
                     e.dma_start(out=xdst, in_=xsrc1).then_inc(d_x.h, 16))
                d_x.n += 16
        split_mark = []
        for k in range(NK):
            xts = xT_sl(k, 0, T)
            P.em("V", lambda e, k=k, xts=xts:
                 e.tensor_copy(xhi[:, k * T:(k + 1) * T], xts))
            P.em("V", lambda e, k=k, xts=xts: e.tensor_tensor(
                xlo[:, k * T:(k + 1) * T], xts, xhi[:, k * T:(k + 1) * T],
                ALU.subtract))
            split_mark.append(v_inc())
        p1_marks[c] = split_mark

    def do_chunk(c):
        if c == 0:
            emit_P1(0)
        split_mark = p1_marks.pop(c)

        if c == 0:
            wait_pre("T", "gates")
        if c > 0:
            P.em("V", lambda e, v=64 * c: e.wait_ge(d_out.h, v))
        for m in range(NK):
            b_ig = (m % 2) * 2
            b_rg = b_ig + 1
            t_wait_bank(b_ig)
            t_wait_bank(b_rg)
            cnt = [0, 0]
            nmm = 3 * NK
            for k in range(NK):
                q = k // 4
                if k % 4 == 0:
                    gRing.emit_wait("T", c * 64 + m * 4 + q)
                if m == 0:
                    P.em("T", lambda e, v=split_mark[k]: e.wait_ge(sV.h, v))
                slot = (c * 64 + m * 4 + q) % 3
                koff = (k % 4) * 512
                hi_in = gring[slot][:, koff:koff + 128]
                lo_in = gring[slot][:, koff + 128:koff + 256]
                hi_g = gring[slot][:, koff + 256:koff + 384]
                lo_g = gring[slot][:, koff + 384:koff + 512]
                rh = xhi[:, k * T:(k + 1) * T]
                rl = xlo[:, k * T:(k + 1) * T]
                for ci, (bk, whi, wlo) in enumerate(((b_ig, hi_in, lo_in),
                                                     (b_rg, hi_g, lo_g))):
                    for wt, rr in ((whi, rh), (whi, rl), (wlo, rh)):
                        st = (cnt[ci] == 0)
                        sp = (cnt[ci] == nmm - 1)
                        cnt[ci] += 1
                        P.em("T", lambda e, bk=bk, wt=wt, rr=rr, st=st, sp=sp:
                             e.matmul(pb[bk], wt, rr, start=st, stop=sp))
                if k % 4 == 3:
                    t_inc()
                    gRing.mark(sT)
            gdone = sT.n

            P.em("S", lambda e, v=gdone: e.wait_ge(sT.h, v))
            if chain_done[0] is not None:
                P.em("S", lambda e, v=chain_done[0]: e.wait_ge(sV.h, v))
            P.em("S", lambda e, m=m, b=b_ig: e.activation(
                bufA, pb[b], AF.Exp, bias=nbin[:, m:m + 1],
                scale=consts[:, 0:1]))
            P.em("S", lambda e, m=m, b=b_rg: e.activation(
                bufB, pb[b], AF.Exp, bias=nbg[:, m:m + 1],
                scale=consts[:, 0:1]))
            ev = s_inc()
            bank_free[b_ig] = (sS, ev)
            bank_free[b_rg] = (sS, ev)
            P.em("V", lambda e, v=ev: e.wait_ge(sS.h, v))
            P.em("V", lambda e: e.tensor_scalar(bufA, bufA, 1.0, None, ALU.add))
            P.em("V", lambda e: e.reciprocal(bufC, bufA))
            v_sync()
            P.em("V", lambda e: e.tensor_scalar(bufB, bufB, 1.0, None, ALU.add))
            P.em("V", lambda e: e.reciprocal(bufA, bufB))
            sigv = v_inc()
            P.em("V", lambda e, v=sigv: e.wait_ge(sV.h, v))
            P.em("S", lambda e, v=sigv: e.wait_ge(sV.h, v))
            P.em("S", lambda e, m=m: e.activation(
                bufB, bufA, AF.Exp, scale=a_e[:, m:m + 1]))
            av = s_inc()
            P.em("V", lambda e, v=av: e.wait_ge(sS.h, v))
            P.em("V", lambda e: e.tensor_tensor(bufD, bufB, bufB, ALU.mult))
            P.em("V", lambda e: e.tensor_scalar(bufD, bufD, -1.0, 1.0,
                                                ALU.mult, ALU.add))
            t2v = v_inc()
            P.em("S", lambda e, v=t2v: e.wait_ge(sV.h, v))
            P.em("S", lambda e: e.activation(bufD, bufD, AF.Ln,
                                             bias=consts[:, 3:4]))
            s_sync()
            P.em("S", lambda e: e.activation(bufD, bufD, AF.Exp,
                                             scale=consts[:, 1:2]))
            bv = s_inc()
            xts = xT_sl(m, 0, T)
            P.em("V", lambda e, v=bv: e.wait_ge(sS.h, v))
            P.em("V", lambda e, xts=xts: e.tensor_tensor(bufE, bufC, xts,
                                                         ALU.mult))
            P.em("V", lambda e: e.tensor_tensor(bufE, bufE, bufD, ALU.mult))
            v_sync()
            if c == 0:
                P.em("V", lambda e: e.tensor_tensor_scan(
                    bufC, bufB, bufE, 0.0, ALU.mult, ALU.add))
            else:
                P.em("V", lambda e, m=m: e.tensor_tensor_scan(
                    bufC, bufB, bufE, carry[:, m:m + 1], ALU.mult, ALU.add))
            v_sync()
            P.em("V", lambda e, m=m: e.tensor_copy(carry[:, m:m + 1],
                                                   bufC[:, T - 1:T]))
            v_sync()
            P.em("V", lambda e, xts=xts: e.tensor_tensor(bufD, bufC, xts,
                                                         ALU.add))
            r1v = v_inc()
            chain_done[0] = r1v
            pend_tr.append((m, r1v))
            if debug_dump and c == 0 and m == 0:
                P.em("G", lambda e, v=r1v: e.wait_ge(sV.h, v))
                for bi, bb in enumerate((bufA, bufB, bufC, bufD, bufE)):
                    P.em("G", lambda e, bb=bb, bi=bi: e.dma_start(
                        out=dbg["dbg_sig"][:, bi * T:(bi + 1) * T], in_=bb)
                        .then_inc(d_out.h, 16))
                    d_out.n += 16
            if len(pend_tr) > 2:
                tr_prev = emit_tr(*pend_tr.pop(0))
                P.em("V", lambda e, v=tr_prev: e.wait_ge(sT.h, v))
            P.em("V", lambda e, hb=m % 2: e.tensor_copy(bufDhold[hb], bufD))
            holdv = v_inc()
            pend_tr[-1] = (m, holdv)
            if c == 0:
                pre_state["pump"](9)
        while pend_tr:
            emit_tr(*pend_tr.pop(0))
        if c == 0:
            pre_state["pump"](10 ** 9)
        yv = sV.n
        if debug_dump and c == 0:
            P.em("G", lambda e, v=yv: e.wait_ge(sV.h, v))
            P.em("G", lambda e: e.dma_start(
                out=dbg["dbg_xT"][:, :8 * T], in_=xTa).then_inc(d_out.h, 16))
            P.em("G", lambda e: e.dma_start(
                out=dbg["dbg_xT"][:, 8 * T:], in_=xTb).then_inc(d_out.h, 16))
            d_out.n += 32
            P.em("V", lambda e, v=d_out.n: e.wait_ge(d_out.h, v))
            P.em("S", lambda e, v=d_out.n: e.wait_ge(d_out.h, v))
            P.em("T", lambda e, v=d_out.n: e.wait_ge(d_out.h, v))
            P.em("G", lambda e: e.dma_start(out=dbg["dbg_y0"], in_=y_t)
                 .then_inc(d_out.h, 16))
            d_out.n += 16
            P.em("V", lambda e, v=d_out.n: e.wait_ge(d_out.h, v))
            P.em("S", lambda e, v=d_out.n: e.wait_ge(d_out.h, v))
            P.em("T", lambda e, v=d_out.n: e.wait_ge(d_out.h, v))

        if c == 0:
            wait_pre("S", "w2")
            wait_pre("V", "w2")
        P.em("S", lambda e, v=yv: e.wait_ge(sV.h, v))
        for tt in range(NT):
            yt = y_t[:, tt * D:(tt + 1) * D]
            P.em("S", lambda e, yt=yt, tt=tt: e.activation(
                sqscr, yt, AF.Square, accum_out=ssq[:, tt:tt + 1]))
        sqv = s_inc()
        P.em("V", lambda e, v=yv: e.wait_ge(sV.h, v))
        for tt in range(NT):
            yt = y_t[:, tt * D:(tt + 1) * D]
            P.em("V", lambda e, yt=yt, tt=tt: e.reduce_sum(
                ssum[:, tt:tt + 1], yt, axis=AX.X))
        v_sync()
        P.em("V", lambda e, v=sqv: e.wait_ge(sS.h, v))
        P.em("V", lambda e: e.tensor_scalar(smean, ssum, 1.0 / D, None,
                                            ALU.mult))
        P.em("V", lambda e: e.tensor_scalar(svar, ssq, 1.0 / D, None,
                                            ALU.mult))
        v_sync()
        P.em("V", lambda e: e.tensor_tensor(stt1, smean, smean, ALU.mult))
        v_sync()
        P.em("V", lambda e: e.tensor_tensor(svar, svar, stt1, ALU.subtract))
        vv = v_inc()
        P.em("S", lambda e, v=vv: e.wait_ge(sV.h, v))
        P.em("S", lambda e: e.activation(sinv, svar, AF.Ln,
                                         bias=consts[:, 4:5]))
        s_sync()
        P.em("S", lambda e: e.activation(sinv, sinv, AF.Exp,
                                         scale=consts[:, 2:3]))
        iv = s_inc()
        P.em("V", lambda e, v=iv: e.wait_ge(sS.h, v))
        P.em("V", lambda e: e.tensor_tensor(snmi, smean, sinv, ALU.mult))
        v_sync()
        P.em("V", lambda e: e.tensor_scalar(snmi, snmi, -1.0, None, ALU.mult))
        nv = v_inc()
        P.em("S", lambda e, v=nv: e.wait_ge(sV.h, v))
        for tt in range(NT):
            yt = y_t[:, tt * D:(tt + 1) * D]
            P.em("S", lambda e, yt=yt, tt=tt: e.activation(
                yt, yt, AF.Identity, bias=snmi[:, tt:tt + 1],
                scale=sinv[:, tt:tt + 1]))
        ynorm = s_inc()
        if debug_dump and c == 0:
            P.em("G", lambda e, v=ynorm: e.wait_ge(sS.h, v))
            for i, stt_ in enumerate((ssum, ssq, smean, svar, sinv, snmi)):
                P.em("G", lambda e, stt_=stt_, i=i: e.dma_start(
                    out=dbg["dbg_sig"][:, 5 * T + i * 8:5 * T + i * 8 + NT],
                    in_=stt_).then_inc(d_out.h, 16))
                d_out.n += 16
            P.em("V", lambda e, v=d_out.n: e.wait_ge(d_out.h, v))
            P.em("S", lambda e, v=d_out.n: e.wait_ge(d_out.h, v))
        P.em("V", lambda e, v=ynorm: e.wait_ge(sS.h, v))
        for tt in range(NT):
            yt = y_t[:, tt * D:(tt + 1) * D]
            P.em("V", lambda e, yt=yt, tt=tt: e.reduce_max(
                sa1[:, tt:tt + 1], yt, axis=AX.X, apply_absolute_value=True))
        v_sync()
        P.em("V", lambda e: e.tensor_scalar_max(sa1, sa1, 1e-5))
        v_sync()
        P.em("V", lambda e: e.tensor_scalar(srs1, sa1, 1.0 / 127.0, None,
                                            ALU.mult))
        P.em("V", lambda e: e.reciprocal(ss1, sa1))
        v_sync()
        P.em("V", lambda e: e.tensor_scalar(ss1, ss1, 127.0, None, ALU.mult))
        v_sync()
        for tt in range(NT):
            yt = y_t[:, tt * D:(tt + 1) * D]
            P.em("V", lambda e, yt=yt, tt=tt: e.tensor_scalar(
                q1f16, yt, ss1[:, tt:tt + 1], MAGIC, ALU.mult, ALU.add))
            P.em("V", lambda e, tt=tt: e.tensor_scalar(
                q1tm[:, tt * D:(tt + 1) * D], q1f16, MAGIC, None,
                ALU.subtract))
        q1v = v_inc()
        P.em("T", lambda e, v=q1v: e.wait_ge(sV.h, v))
        q1f_mark = []
        for k in range(NK):
            b = 4 + (k % 2)
            t_wait_bank(b)
            for tt in range(NT):
                src = q1tm[:, tt * D + k * 128: tt * D + k * 128 + 128]
                P.em("T", lambda e, src=src, b=b, tt=tt: e.matmul(
                    pb[b][:, tt * 128:(tt + 1) * 128], src, id32,
                    start=(tt == 0), stop=(tt == NT - 1), is_transpose=True))
            tv = t_inc()
            P.em("V", lambda e, v=tv: e.wait_ge(sT.h, v))
            P.em("V", lambda e, k=k, b=b: e.tensor_copy(
                q1F[:, k * T:(k + 1) * T], pb[b]))
            q1f_mark.append(v_inc())
            bank_free[b] = (sV, q1f_mark[-1])
        q1fv = sV.n
        if debug_dump and c == 0:
            P.em("G", lambda e, v=q1fv: e.wait_ge(sV.h, v))
            P.em("G", lambda e: e.dma_start(out=dbg["dbg_q1F"],
                                            in_=q1F.astype(F32)
                                            if False else q1F)
                 .then_inc(d_out.h, 16))
            d_out.n += 16
            P.em("G", lambda e: e.dma_start(out=dbg["dbg_y1"], in_=y_t)
                 .then_inc(d_out.h, 16))
            d_out.n += 16
            P.em("V", lambda e, v=d_out.n: e.wait_ge(d_out.h, v))
            P.em("S", lambda e, v=d_out.n: e.wait_ge(d_out.h, v))
            P.em("T", lambda e, v=d_out.n: e.wait_ge(d_out.h, v))

        if c == 0:
            for qe in ("Q", "S", "G"):
                wait_pre(qe, "w13")
        for j in range(NJ):
            for k in range(NK):
                src = s13_d[j, k]
                def mkload(src=src):
                    def fn(eng, slot, ssem):
                        P.em(eng, lambda e, src=src, slot=slot, ss=ssem:
                             e.dma_start(out=wring[slot], in_=src)
                             .then_inc(ss.h, 16))
                    return fn
                wRing.push(mkload())
        wRing.pump()

        wseq0 = wRing.seq_marked
        for j in range(NJ):
            for b in range(8):
                t_wait_bank(b)
            first = [True] * 8
            for k in range(NK):
                seq = wseq0 + j * NK + k
                wRing.emit_wait("T", seq)
                if j == 0:
                    P.em("T", lambda e, v=q1f_mark[k]: e.wait_ge(sV.h, v))
                slot = seq % 3
                hi0 = wring[slot][:, 0:512]
                lo0 = wring[slot][:, 512:1024]
                hi1 = wring[slot][:, 1024:1536]
                lo1 = wring[slot][:, 1536:2048]
                for m in range(NT):
                    lhs = q1F[:, k * T + m * 128: k * T + m * 128 + 128]
                    for hb, whi, wlo in ((0, hi0, lo0), (1, hi1, lo1)):
                        bk = m * 2 + hb
                        st = first[bk]
                        first[bk] = False
                        sp = (k == NK - 1)
                        P.em("T", lambda e, bk=bk, lhs=lhs, w=whi, st=st:
                             e.matmul(pb[bk], lhs, w, start=st, stop=False))
                        P.em("T", lambda e, bk=bk, lhs=lhs, w=wlo, sp=sp:
                             e.matmul(pb[bk], lhs, w, start=False, stop=sp))
                t_inc()
                wRing.mark(sT)
            jv = sT.n
            P.em("S", lambda e, v=jv: e.wait_ge(sT.h, v))
            for m in range(NT):
                d0 = drbuf[0] if m % 2 == 0 else q2bf[0]
                d1 = drbuf[1] if m % 2 == 0 else q2bf[1]
                if dr_done[m % 2] is not None:
                    P.em("S", lambda e, v=dr_done[m % 2]: e.wait_ge(sV.h, v))
                P.em("S", lambda e, m=m, d0=d0: e.activation(
                    d0, pb[m * 2], AF.Relu, scale=srs1[:, m:m + 1]))
                P.em("S", lambda e, m=m, d1=d1: e.activation(
                    d1, pb[m * 2 + 1], AF.Identity, scale=srs1[:, m:m + 1]))
                dv = s_inc()
                bank_free[m * 2] = (sS, dv)
                bank_free[m * 2 + 1] = (sS, dv)
                gs = gnw[:, (j * 4 + m) * 512:(j * 4 + m + 1) * 512]
                P.em("V", lambda e, v=dv: e.wait_ge(sS.h, v))
                P.em("V", lambda e, g=gs, d0=d0: e.tensor_tensor(
                    g, d0, d0, ALU.mult))
                P.em("V", lambda e, g=gs, d1=d1: e.tensor_tensor(
                    g, g, d1, ALU.mult))
                P.em("V", lambda e, g=gs, d0=d0: e.tensor_tensor(
                    d0, g, g, ALU.mult))
                P.em("V", lambda e, m=m, j=j, d0=d0: e.reduce_sum(
                    sumb[:, m * NJ + j:m * NJ + j + 1], d0, axis=AX.X))
                P.em("V", lambda e, g=gs, j=j: e.tensor_tensor(
                    g, g, nwrep[:, j * 512:(j + 1) * 512], ALU.mult))
                P.em("V", lambda e, g=gs, m=m, j=j: e.reduce_max(
                    maxb[:, m * NJ + j:m * NJ + j + 1], g, axis=AX.X,
                    apply_absolute_value=True))
                dr_done[m % 2] = v_inc()

        v_sync()
        for m in range(NT):
            P.em("V", lambda e, m=m: e.reduce_sum(
                sms[:, m:m + 1], sumb[:, m * NJ:(m + 1) * NJ], axis=AX.X))
            P.em("V", lambda e, m=m: e.reduce_max(
                samax[:, m:m + 1], maxb[:, m * NJ:(m + 1) * NJ], axis=AX.X))
        v_sync()
        rdv = sV.n
        P.em("S", lambda e, v=rdv: e.wait_ge(sV.h, v))
        P.em("S", lambda e: e.activation(srstd, sms, AF.Ln,
                                         bias=consts[:, 4:5],
                                         scale=consts[:, 5:6]))
        s_sync()
        P.em("S", lambda e: e.activation(srstd, srstd, AF.Exp,
                                         scale=consts[:, 2:3]))
        rsv = s_inc()
        P.em("V", lambda e, v=rsv: e.wait_ge(sS.h, v))
        P.em("V", lambda e: e.tensor_tensor(stt1, samax, srstd, ALU.mult))
        v_sync()
        P.em("V", lambda e: e.tensor_scalar_max(stt1, stt1, 1e-5))
        v_sync()
        P.em("V", lambda e: e.tensor_scalar(srr2, stt1, 1.0 / 127.0, None,
                                            ALU.mult))
        P.em("V", lambda e: e.reciprocal(ss2c, stt1))
        v_sync()
        P.em("V", lambda e: e.tensor_scalar(ss2c, ss2c, 127.0, None, ALU.mult))
        v_sync()
        P.em("V", lambda e: e.tensor_tensor(srs2, srstd, ss2c, ALU.mult))
        v_sync()
        qtr = {}
        for j in range(NJ):
            for m in range(NT):
                idx = j * 4 + m
                gs = gnw[:, idx * 512:(idx + 1) * 512]
                if idx >= 4:
                    P.em("V", lambda e, v=qtr[idx - 4]: e.wait_ge(sT.h, v))
                P.em("V", lambda e, g=gs, m=m: e.tensor_scalar(
                    q2f16, g, srs2[:, m:m + 1], MAGIC, ALU.mult, ALU.add))
                if idx == 39:
                    r1_free[0] = (sV, v_inc())
                P.em("V", lambda e, idx=idx: e.tensor_scalar(
                    q2bf[idx % 4], q2f16, MAGIC, None, ALU.subtract))
                qv = v_inc()
                b = idx % 2
                t_wait_bank(b)
                P.em("T", lambda e, v=qv: e.wait_ge(sV.h, v))
                for s in range(4):
                    P.em("T", lambda e, s=s, b=b, idx=idx: e.matmul(
                        pb[b][:, s * 128:(s + 1) * 128],
                        q2bf[idx % 4][:, s * 128:(s + 1) * 128], id32,
                        start=(s == 0), stop=(s == 3), is_transpose=True))
                tv = t_inc()
                qtr[idx] = tv
                dst = iqT[:, idx * 512:(idx + 1) * 512]
                P.em("S", lambda e, v=tv: e.wait_ge(sT.h, v))
                P.em("S", lambda e, dst=dst, b=b: e.activation(
                    dst, pb[b], AF.Copy))
                bank_free[b] = (sS, s_inc())
        lastS = sS.n
        P.em("V", lambda e, v=lastS: e.wait_ge(sS.h, v))
        iqv = v_inc()

        if c + 1 < chunks:
            emit_P1(c + 1, tguard=qtr[NJ * 4 - 1])

        if c == 0:
            for qe in ("Q", "S", "G"):
                wait_pre(qe, "w2")
        for dh in range(2):
            for K in range(NKF):
                src = s2_d[dh, K]
                def mkload(src=src):
                    def fn(eng, slot, ssem):
                        P.em(eng, lambda e, src=src, slot=slot, ss=ssem:
                             e.dma_start(out=wring[slot], in_=src)
                             .then_inc(ss.h, 16))
                    return fn
                wRing.push(mkload())
        wRing.pump()

        P.em("T", lambda e, v=iqv: e.wait_ge(sV.h, v))
        wseq1 = wRing.seq_marked
        for dh in range(2):
            for b in range(8):
                t_wait_bank(b)
            for K in range(NKF):
                seq = wseq1 + dh * NKF + K
                wRing.emit_wait("T", seq)
                slot = seq % 3
                whis = [wring[slot][:, b2 * 512:(b2 + 1) * 512]
                        for b2 in range(2)]
                wlos = [wring[slot][:, 1024 + b2 * 512:1024 + (b2 + 1) * 512]
                        for b2 in range(2)]
                for m in range(NT):
                    koff = ((K // 4) * 4 + m) * 512 + (K % 4) * 128
                    lhs = iqT[:, koff:koff + 128]
                    for b2 in range(2):
                        bk = m * 2 + b2
                        P.em("T", lambda e, bk=bk, lhs=lhs, w=whis[b2],
                             st=(K == 0): e.matmul(pb[bk], lhs, w, start=st,
                                                   stop=False))
                        P.em("T", lambda e, bk=bk, lhs=lhs, w=wlos[b2],
                             sp=(K == NKF - 1): e.matmul(pb[bk], lhs, w,
                                                         start=False, stop=sp))
                t_inc()
                wRing.mark(sT)
            fv = sT.n
            P.em("V", lambda e, v=fv: e.wait_ge(sT.h, v))
            for m in range(NT):
                for b2 in range(2):
                    bk = m * 2 + b2
                    ydst = y_t[:, m * D + dh * 1024 + b2 * 512:
                               m * D + dh * 1024 + (b2 + 1) * 512]
                    P.em("V", lambda e, bk=bk, ydst=ydst, m=m:
                         e.scalar_tensor_tensor(ydst, pb[bk],
                                                srr2[:, m:m + 1], ydst,
                                                ALU.mult, ALU.add))
                    bank_free[bk] = (sV, v_inc())
        v_sync()
        r2v = sV.n

        if c == 0:
            wait_pre("S", "w2")
        P.em("S", lambda e, v=r2v: e.wait_ge(sV.h, v))
        for tt in range(NT):
            yt = y_t[:, tt * D:(tt + 1) * D]
            P.em("S", lambda e, yt=yt, tt=tt: e.activation(
                sqscr2, yt, AF.Square, accum_out=ssq[:, tt:tt + 1]))
        sqv2 = s_inc()
        P.em("V", lambda e, v=r2v: e.wait_ge(sV.h, v))
        for tt in range(NT):
            yt = y_t[:, tt * D:(tt + 1) * D]
            P.em("V", lambda e, yt=yt, tt=tt: e.reduce_sum(
                ssum[:, tt:tt + 1], yt, axis=AX.X))
        v_sync()
        P.em("V", lambda e, v=sqv2: e.wait_ge(sS.h, v))
        P.em("V", lambda e: e.tensor_scalar(smean, ssum, 1.0 / D, None,
                                            ALU.mult))
        P.em("V", lambda e: e.tensor_scalar(svar, ssq, 1.0 / D, None,
                                            ALU.mult))
        v_sync()
        P.em("V", lambda e: e.tensor_tensor(stt1, smean, smean, ALU.mult))
        v_sync()
        P.em("V", lambda e: e.tensor_tensor(svar, svar, stt1, ALU.subtract))
        vv2 = v_inc()
        P.em("S", lambda e, v=vv2: e.wait_ge(sV.h, v))
        P.em("S", lambda e: e.activation(sinv, svar, AF.Ln,
                                         bias=consts[:, 4:5]))
        s_sync()
        P.em("S", lambda e: e.activation(sinv, sinv, AF.Exp,
                                         scale=consts[:, 2:3]))
        iv2 = s_inc()
        P.em("V", lambda e, v=iv2: e.wait_ge(sS.h, v))
        P.em("V", lambda e: e.tensor_tensor(snmi, smean, sinv, ALU.mult))
        v_sync()
        P.em("V", lambda e: e.tensor_scalar(snmi, snmi, -1.0, None, ALU.mult))
        nv2 = v_inc()
        P.em("S", lambda e, v=nv2: e.wait_ge(sV.h, v))
        for tt in range(NT):
            yt = y_t[:, tt * D:(tt + 1) * D]
            P.em("S", lambda e, yt=yt, tt=tt: e.activation(
                yt, yt, AF.Identity, bias=snmi[:, tt:tt + 1],
                scale=sinv[:, tt:tt + 1]))
            ov = s_inc()
            odst = out_d[c * T + tt * 128:c * T + (tt + 1) * 128, :]
            P.em("G", lambda e, v=ov: e.wait_ge(sS.h, v))
            P.em("G", lambda e, odst=odst, yt=yt:
                 e.dma_start(out=odst, in_=yt).then_inc(d_out.h, 16))
            d_out.n += 16

    setup()
    prepass()
    for c in range(chunks):
        do_chunk(c)
    P.em("G", lambda e, v=d_out.n: e.wait_ge(d_out.h, v))
    P.do_splice()

    with nc.Block() as block:
        @block.tensor
        def _(eng):
            for fn in P.streams["T"]:
                fn(eng)

        @block.vector
        def _(eng):
            for fn in P.streams["V"]:
                fn(eng)

        @block.scalar
        def _(eng):
            for fn in P.streams["S"]:
                fn(eng)

        @block.sync
        def _(eng):
            for fn in P.streams["Q"]:
                fn(eng)

        @block.gpsimd
        def _(eng):
            for fn in P.streams["G"]:
                fn(eng)

    return nc


_NC_CACHE = None


def kernel(L0_attn_x, L0_attn_W_in, L0_attn_b_in, L0_attn_W_gate,
           L0_attn_b_gate, L0_attn_forget_lambda, L0_ffn_w13, L0_ffn_w2,
           L0_ffn_norm_w):
    global _NC_CACHE
    import ml_dtypes
    x = np.ascontiguousarray(np.asarray(L0_attn_x, dtype=np.float32))
    G = np.stack([np.asarray(L0_attn_W_in, np.float32),
                  np.asarray(L0_attn_W_gate, np.float32)])
    wgp = np.ascontiguousarray(
        G.reshape(2, NK, 128, 4, 4, 128).transpose(1, 3, 5, 0, 4, 2)
        .reshape(NK, 4, 128, 1024))
    w13T = np.ascontiguousarray(np.asarray(L0_ffn_w13, np.float32).T)
    w2T = np.ascontiguousarray(np.asarray(L0_ffn_w2, np.float32).T)
    bin_c = np.ascontiguousarray(
        np.asarray(L0_attn_b_in, np.float32).reshape(NK, 128).T)
    bg_c = np.ascontiguousarray(
        np.asarray(L0_attn_b_gate, np.float32).reshape(NK, 128).T)
    lam_c = np.ascontiguousarray(
        np.asarray(L0_attn_forget_lambda, np.float32).reshape(NK, 128).T)
    nw_r = np.ascontiguousarray(np.broadcast_to(
        np.asarray(L0_ffn_norm_w, np.float32).reshape(1, FHALF),
        (128, FHALF)))
    id32 = np.eye(128, dtype=np.float32)
    id16 = np.eye(128, dtype=ml_dtypes.bfloat16)

    if _NC_CACHE is None:
        _NC_CACHE = build_program()
    nc = _NC_CACHE

    shared = dict(wg_proto=wgp, w13T=w13T, w2T=w2T, b_in_col=bin_c,
                  b_g_col=bg_c, lam_col=lam_c, normw_rep_in=nw_r, ident32=id32,
                  ident16=id16)
    in_maps = [dict(shared, x=np.ascontiguousarray(x[b])) for b in range(8)]
    res = run_bass_kernel_spmd(nc, in_maps, core_ids=list(range(8)))
    return np.stack([r["out"] for r in res.results], axis=0)
